# version 19
# baseline (speedup 1.0000x reference)
import contextlib
import math

import numpy as np
import ml_dtypes
import concourse.bass as bass
import concourse.mybir as mybir
from concourse.bass_utils import run_bass_kernel_spmd

F32 = mybir.dt.float32
BF16 = mybir.dt.bfloat16
AF = mybir.ActivationFunctionType
ALU = mybir.AluOpType
AX = mybir.AxisListType

E = 1024
T = 4096
H = 16
DH = 64
NE = 15
TH = 2048
NEG = -1.0e30
WIN_C = 60.0

ENGS = ("tensor", "vector", "scalar", "sync", "gpsimd")


def _slopes16():
    start = 2.0 ** (-2.0 ** (-(math.log2(16) - 3)))
    return np.array([start * start ** i for i in range(16)], dtype=np.float32)


def _tri_split(v):
    hi = v.astype(ml_dtypes.bfloat16).astype(np.float64)
    r = v - hi
    mid = r.astype(ml_dtypes.bfloat16).astype(np.float64)
    lo = (r - mid).astype(ml_dtypes.bfloat16).astype(np.float64)
    return np.stack([hi, mid, lo]).astype(np.float32)


class Sched:
    def __init__(self, nc):
        self.nc = nc
        self.ops = {e: [] for e in ENGS}
        self.cnt = {}

    def sem(self, handle):
        self.cnt[handle.name] = 0
        return handle

    def op(self, eng, method, *args, inc=None, **kw):
        incs = []
        if inc:
            for s, n in inc:
                self.cnt[s.name] += n
                incs.append((s, n))
        self.ops[eng].append(("op", method, args, kw, incs))

    def mark(self, sem):
        return self.cnt[sem.name]

    def wait(self, eng, sem, val):
        if val > 0:
            self.ops[eng].append(("wait", sem, val))

    def wait_cur(self, eng, sem):
        self.wait(eng, sem, self.cnt[sem.name])

    def replay(self, block):
        for eng in ENGS:
            lst = self.ops[eng]

            def body(handle, lst=lst):
                for d in lst:
                    if d[0] == "op":
                        _, method, args, kw, incs = d
                        ins = getattr(handle, method)(*args, **kw)
                        for s, n in incs:
                            ins.then_inc(s, n)
                    else:
                        handle.wait_ge(d[1], d[2])
            getattr(block, eng)(body)


class DmaStream:
    def __init__(self, sched, eng, sems):
        self.s, self.eng, self.sems, self.n = sched, eng, sems, 0

    def dma(self, out, in_, **kw):
        sem = self.sems[self.n % len(self.sems)]
        self.s.op(self.eng, "dma_start", inc=[(sem, 16)], out=out, in_=in_, **kw)
        self.n += 1
        return self.n - 1

    def wait_for(self, eng, idx):
        k = len(self.sems)
        self.s.wait(eng, self.sems[idx % k], 16 * (idx // k + 1))

    def wait_all(self, eng):
        for i, sem in enumerate(self.sems):
            tot = (self.n - 1 - i) // len(self.sems) + 1 if self.n > i else 0
            if tot:
                self.s.wait(eng, sem, 16 * tot)


def build_program(parts=("ffn1", "attn", "ffn2")):
    nc = bass.Bass("TRN2", target_bir_lowering=False, debug=False)
    dt = {}

    def din(name, shape):
        dt[name] = nc.dram_tensor(name, list(shape), F32, kind="ExternalInput").ap()

    din("xT", (E, T))
    din("w1s", (NE, 16, 128, 8, 128))
    din("w2s", (NE, 8, 128, 8, 128))
    din("ws1s", (32, 128, 8, 128))
    din("ws2s", (8, 128, 16, 128))
    din("gws", (128, 8, 15))
    din("wqks", (H, 128, 8, 128))
    din("wvs", (4, 128, 8, 4 * DH))
    din("wos", (8, 128, 8, 128))
    din("fc1b", (128, NE * 16))
    din("fc2bT", (NE, E))
    din("gmult", (128, NE * 8))
    din("sfc1b", (128, 32))
    din("sgm", (128, 16))
    din("sfc2b", (128, 8))
    din("bqkv", (128, H))
    din("selqv", (128, 128))
    din("bvrow", (1, E))
    din("bov", (128, 8))
    din("alq", (H, 3, T))
    din("alk", (H, 3, T))
    din("masks", (4, 128, 512))
    din("identb", (128, 128))
    din("sentv", (128, 8))
    outT = nc.dram_tensor("outT", [E, T], F32, kind="ExternalOutput").ap()
    z1T = nc.dram_tensor("z1T", [E, T], F32, kind="Internal").ap()
    z2T = nc.dram_tensor("z2T", [E, T], F32, kind="Internal").ap()

    chain = [dt["xT"], z1T, z2T]
    io = {}
    for i, p in enumerate(parts):
        io[p] = (chain[i], outT if i == len(parts) - 1 else chain[i + 1])

    es = contextlib.ExitStack()

    def sb(name, shape, dtype):
        return es.enter_context(nc.sbuf_tensor(name, list(shape), dtype))

    bqk_sb = sb("bqk_sb", (128, H), F32)
    selq_sb = sb("selq_sb", (128, 128), BF16)
    bo_sb = sb("bo_sb", (128, 8), F32)
    ones_r32 = sb("ones_r32", (65, 128), F32)
    ones_rbf = sb("ones_rbf", (1, 128), BF16)
    ones_cbf = sb("ones_cbf", (128, 1), BF16)
    eps_sb = sb("eps_sb", (1, 1), F32)

    S = Sched(nc)

    def mksem(name):
        return S.sem(es.enter_context(nc.semaphore(name)))

    sem_t = mksem("sem_t")
    sem_v = mksem("sem_v")
    sem_s = mksem("sem_s")
    sem_w = [mksem(f"sem_w{i}") for i in range(4)]
    sem_zl = [mksem(f"sem_zl{i}") for i in range(2)]
    sem_st = mksem("sem_st")
    wstream = DmaStream(S, "gpsimd", sem_w)
    zstream = DmaStream(S, "sync", sem_zl)
    n_stores = [0]

    def store(out, in_):
        S.op("sync", "dma_start", inc=[(sem_st, 16)], out=out, in_=in_)
        n_stores[0] += 1

    def light_barrier():
        for e in ENGS:
            for sem in (sem_t, sem_v, sem_s):
                S.wait_cur(e, sem)
            wstream.wait_all(e)
            zstream.wait_all(e)
            S.wait(e, sem_st, 16 * n_stores[0])

    for dst, src in [(bqk_sb, "bqkv"), (selq_sb, "selqv"),
                     (bo_sb, "bov")]:
        wstream.dma(dst[:], dt[src])
    wstream.wait_all("vector")
    S.op("vector", "memset", ones_r32[:], 1.0)
    S.op("vector", "memset", ones_rbf[:], 1.0)
    S.op("vector", "memset", ones_cbf[:], 1.0)
    S.op("vector", "memset", eps_sb[:], 1e-5, inc=[(sem_v, 1)])
    for e in ("tensor", "scalar", "sync", "gpsimd"):
        wstream.wait_all(e)
        S.wait_cur(e, sem_v)

    def emit_ln_chunk(z_in, tc, xn_dst, B):
        p = tc % 2
        zbf, zsq = B["zbf"][p], B["zsq"][p]
        bc0, bc1 = B["psBc"][p]
        st1, st2 = bc0[0:1, 0:512], bc1[0:1, 0:512]
        mrow, vrow, m2row = B["rows"][p]
        tmp = B["tmp"][p]
        mv, mt = B["mv"], B["mt"]
        src = z_in.rearrange("(dt p) t -> p dt t", p=128)[:, :,
                                                          tc * 512:(tc + 1) * 512]
        stgL = B.get("stgL")
        if stgL is not None:
            sl = stgL[p]
            if tc >= 2:
                S.wait("sync", sem_v, mv[tc - 2])
            S.wait("sync", sem_st, 16 * n_stores[0])
            idx = zstream.dma(sl.rearrange("p (dt t) -> p dt t", dt=8), src)
            zstream.wait_for("scalar", idx)
            if tc >= 2:
                S.wait("scalar", sem_t, mt[tc - 2])
            S.op("scalar", "copy", out=zbf[:], in_=sl[:])
            S.op("scalar", "activation", out=zsq[:], in_=sl[:],
                 func=AF.Square, inc=[(sem_s, 1)])
            sub_src = sl
        else:
            if tc >= 2:
                S.wait("gpsimd", sem_v, mv[tc - 2])
            idx = wstream.dma(zbf.rearrange("p (dt t) -> p dt t", dt=8), src)
            wstream.wait_for("scalar", idx)
            if tc >= 2:
                S.wait("scalar", sem_t, mt[tc - 2])
            S.op("scalar", "activation", out=zsq[:], in_=zbf[:],
                 func=AF.Square, inc=[(sem_s, 1)])
            sub_src = zbf
        sq_m = S.mark(sem_s)
        S.wait("tensor", sem_s, sq_m)
        if tc >= 2:
            S.wait("tensor", sem_v, mv[tc - 2])
        for d in range(8):
            S.op("tensor", "matmul", st1, ones_cbf[:],
                 zbf[:, d * 512:(d + 1) * 512], start=(d == 0), stop=(d == 7))
        for d in range(8):
            S.op("tensor", "matmul", st2, ones_cbf[:],
                 zsq[:, d * 512:(d + 1) * 512], start=(d == 0), stop=(d == 7),
                 inc=[(sem_t, 1)] if d == 7 else None)
        mt[tc] = S.mark(sem_t)
        S.wait("vector", sem_t, mt[tc])
        S.op("vector", "tensor_scalar_mul", mrow, st1, 1.0 / 1024.0)
        S.op("vector", "tensor_tensor", m2row, mrow, mrow, ALU.mult)
        S.op("vector", "scalar_tensor_tensor", vrow, st2, 1.0 / 1024.0,
             m2row, ALU.mult, ALU.subtract, inc=[(sem_v, 1)])
        row_m = S.mark(sem_v)
        S.wait("scalar", sem_v, row_m)
        S.op("scalar", "activation", out=vrow, in_=vrow,
             func=AF.Ln, bias=eps_sb[:], scale=1.0)
        S.op("scalar", "activation", out=vrow, in_=vrow,
             func=AF.Exp, scale=-0.5, inc=[(sem_s, 1)])
        ex_m = S.mark(sem_s)
        lm, lv = B["lhs"][p]
        S.wait("tensor", sem_s, ex_m)
        S.op("tensor", "matmul", bc0[:, 0:512], lm, mrow,
             start=True, stop=True)
        S.op("tensor", "matmul", bc1[:, 0:512], lv, vrow,
             start=True, stop=True, inc=[(sem_t, 1)])
        bc_m = S.mark(sem_t)
        S.wait("vector", sem_t, bc_m)
        for d in range(8):
            S.op("vector", "tensor_tensor", tmp[:],
                 sub_src[:, d * 512:(d + 1) * 512], bc0[:, 0:512],
                 ALU.subtract)
            S.op("vector", "tensor_tensor", xn_dst(d), tmp[:],
                 bc1[:, 0:512], ALU.mult,
                 inc=[(sem_v, 1)] if d == 7 else None)
        mv[tc] = S.mark(sem_v)

    def emit_ffn(z_in, z_out, tag):
        ff = contextlib.ExitStack()

        def fsb(name, shape, dtype):
            return ff.enter_context(nc.sbuf_tensor(f"{tag}{name}", list(shape), dtype))

        def fps(name, shape):
            return ff.enter_context(nc.psum_tensor(f"{tag}{name}", list(shape), F32))

        xn = fsb("xn", (128, 8 * TH), BF16)
        ysb = fsb("y", (128, 8 * TH), F32)
        a_sb = fsb("a", (128, 8 * TH), BF16)
        stg = fsb("stg", (128, 4096), F32)
        fc1b_sb = fsb("fc1b_sb", (128, NE * 16), F32)
        gm_sb = fsb("gm_sb", (128, NE * 8), F32)
        sfc1b_sb = fsb("sfc1b_sb", (128, 32), F32)
        sgm_sb = fsb("sgm_sb", (128, 16), F32)
        sfc2b_sb = fsb("sfc2b_sb", (128, 8), F32)
        gate_sb = fsb("gate_sb", (128, 8 * 15), BF16)
        ident_sb = fsb("ident_sb", (128, 128), BF16)
        sent_sb = fsb("sent_sb", (128, 8), F32)
        w1r = fsb("w1r", (128, 6 * 1024), BF16)
        w2r = fsb("w2r", (128, 4 * 1024), BF16)
        comb_b = [fsb(f"combb{i}", (128, TH), BF16) for i in range(2)]
        combrow = fsb("combrow", (1, TH), BF16)
        comb_T = fsb("comb_T", (16, TH), BF16)
        fc2bT_sb = fsb("fc2bT_sb", (NE, E), BF16)
        comb_tok = fsb("comb_tok", (128, 16 * 15), BF16)
        mb = fsb("mb", (128, 512), F32)
        rstdb = fsb("rstdb", (128, 512), F32)
        tmpf = fsb("tmpf", (128, 512), F32)
        g_act = [fsb(f"gact{i}", (128, 512), BF16) for i in range(3)]
        tmp_a = [fsb(f"tmpa{i}", (128, 512), BF16) for i in range(3)]
        lgt_all = fsb("lgt_all", (128, 240), F32)
        work = fsb("work", (128, 23), F32)
        wout = fsb("wout", (128, 23), F32)
        m8 = fsb("m8", (128, 8), F32)
        rep = fsb("rep", (128, 8), F32)
        sel = fsb("sel", (128, 15), F32)
        exm = fsb("exm", (128, 15), F32)
        negm = fsb("negm", (128, 1), F32)
        gsum = fsb("gsum", (128, 1), F32)
        ginv = fsb("ginv", (128, 1), F32)

        psA = [fps(f"psA{i}", (128, 512)) for i in range(2)]
        psB = [fps(f"psB{i}", (128, 512)) for i in range(2)]
        psY = [fps(f"psY{i}", (128, 512)) for i in range(2)]
        aux = fps("aux", (128, 1024))

        lnB = dict(
            stgL=[stg[:, 0:4096], ysb[:, 0:4096]],
            zbf=[a_sb[:, 0:4096], a_sb[:, 8192:12288]],
            zsq=[a_sb[:, 4096:8192], a_sb[:, 12288:16384]],
            psBc=[(aux[:, 0:512], aux[:, 512:1024]),
                  (psA[0][:], psA[1][:])],
            rows=[(rstdb[0:1, 0:512], mb[0:1, 0:512], tmpf[0:1, 0:512]),
                  (tmpf[64:65, 0:512], tmpf[32:33, 0:512],
                   tmpf[96:97, 0:512])],
            lhs=[(ones_r32[0:1, :], ones_r32[0:1, :]),
                 (ones_r32[64:65, :], ones_r32[32:33, :])],
            tmp=[mb[:], rstdb[:]], mv={}, mt={})

        units = [(1, 0), (2, 0)] + [(0, e) for e in range(NE)]

        def w1_src(kind, e, jp, ab):
            if kind == 0:
                return dt["w1s"][e, jp + 8 * ab].rearrange("p d f -> p (d f)")
            j = (jp + 8 * (kind - 1)) + 16 * ab
            return dt["ws1s"][j].rearrange("p d f -> p (d f)")

        def w2_src(kind, e, db):
            if kind == 0:
                return dt["w2s"][e, db].rearrange("p f d -> p (f d)")
            f0 = 8 * (kind - 1)
            return dt["ws2s"][db, :, f0:f0 + 8, :].rearrange("p f d -> p (f d)")

        def bias_cols(kind, e, jp):
            if kind == 0:
                return (fc1b_sb[:, e * 16 + jp:e * 16 + jp + 1],
                        fc1b_sb[:, e * 16 + 8 + jp:e * 16 + 9 + jp],
                        gm_sb[:, e * 8 + jp:e * 8 + jp + 1])
            base = 8 * (kind - 1)
            return (sfc1b_sb[:, base + jp:base + jp + 1],
                    sfc1b_sb[:, 16 + base + jp:17 + base + jp],
                    sgm_sb[:, base + jp:base + jp + 1])

        for dst, src in [(fc1b_sb, "fc1b"), (gm_sb, "gmult"),
                         (sfc1b_sb, "sfc1b"), (sgm_sb, "sgm"),
                         (sfc2b_sb, "sfc2b"), (gate_sb, "gws"),
                         (ident_sb, "identb"), (sent_sb, "sentv")]:
            ap = dt[src]
            if src == "gws":
                ap = ap.rearrange("p d e -> p (d e)")
            wstream.dma(dst[:], ap)
        for e in ("tensor", "vector", "scalar"):
            wstream.wait_all(e)
        fc2bT_dma = wstream.dma(fc2bT_sb[:], dt["fc2bT"])
        wstream.wait_for("tensor", fc2bT_dma)
        for half in range(2):
            light_barrier()

            for c in range(4):
                tc = half * 4 + c

                def xn_dst(d, c=c):
                    return xn[:, d * TH + c * 512:d * TH + (c + 1) * 512]
                emit_ln_chunk(z_in, tc, xn_dst, lnB)
            xn_ready = S.mark(sem_v)
            S.op("vector", "tensor_copy", work[:, 15:23], sent_sb[:],
                 inc=[(sem_v, 1)])

            S.wait("tensor", sem_v, xn_ready)
            for tt in range(16):
                for d in range(8):
                    S.op("tensor", "matmul", aux[:, tt * 15:(tt + 1) * 15],
                         xn[:, d * TH + tt * 128:d * TH + (tt + 1) * 128],
                         gate_sb[:, d * 15:(d + 1) * 15],
                         start=(d == 0), stop=(d == 7),
                         inc=[(sem_t, 1)] if (tt == 15 and d == 7) else None)
            S.wait_cur("scalar", sem_t)
            S.op("scalar", "copy", out=lgt_all[:], in_=aux[:, 0:240],
                 inc=[(sem_s, 1)])
            S.wait_cur("vector", sem_s)
            for tt in range(16):
                lgt = lgt_all[:, tt * 15:(tt + 1) * 15]
                S.op("vector", "tensor_copy", work[:, 0:15], lgt,
                     inc=[(sem_v, 1)])
                S.wait_cur("vector", sem_v)
                S.op("vector", "max", out=m8[:], in_=work[:], inc=[(sem_v, 1)])
                S.wait_cur("vector", sem_v)
                S.op("vector", "tensor_copy", rep[:, 0:3], m8[:, 0:3])
                S.op("vector", "tensor_copy", rep[:, 3:8], sent_sb[:, 0:5],
                     inc=[(sem_v, 1)])
                S.wait_cur("vector", sem_v)
                S.op("vector", "match_replace", out=wout[:], in_to_replace=rep[:],
                     in_values=work[:], imm_value=NEG, inc=[(sem_v, 1)])
                S.wait_cur("vector", sem_v)
                S.op("vector", "tensor_single_scalar", sel[:], wout[:, 0:15],
                     -1e29, ALU.is_lt)
                S.op("vector", "tensor_scalar_mul", negm[:], m8[:, 0:1], -1.0,
                     inc=[(sem_v, 1)])
                S.wait_cur("scalar", sem_v)
                S.op("scalar", "activation", out=exm[:], in_=lgt,
                     func=AF.Exp, bias=negm[:], scale=1.0, inc=[(sem_s, 1)])
                S.wait_cur("vector", sem_s)
                S.op("vector", "tensor_tensor", exm[:], exm[:], sel[:], ALU.mult,
                     inc=[(sem_v, 1)])
                S.wait_cur("vector", sem_v)
                S.op("vector", "reduce_sum", gsum[:], exm[:], axis=AX.X,
                     inc=[(sem_v, 1)])
                S.wait_cur("vector", sem_v)
                S.op("vector", "reciprocal", ginv[:], gsum[:], inc=[(sem_v, 1)])
                S.wait_cur("vector", sem_v)
                S.op("vector", "tensor_single_scalar",
                     comb_tok[:, tt * 15:(tt + 1) * 15], exm[:], ginv[:],
                     ALU.mult, inc=[(sem_v, 1)])
            gating_done = S.mark(sem_v)

            w1_seq = [(u, jp, ab) for u in range(17) for jp in range(8)
                      for ab in range(2)]
            w2_seq = [(u, db) for u in range(17) for db in range(8)]
            w1_dma, w2_dma, w1_free_at, w2_free_at = {}, {}, {}, {}
            w1_next, w2_next = [0], [0]

            def pump_w1(n=1):
                for _ in range(n):
                    seq = w1_next[0]
                    if seq >= len(w1_seq):
                        return
                    u, jp, ab = w1_seq[seq]
                    kind, e = units[u]
                    if seq >= 6:
                        S.wait("gpsimd", sem_t, w1_free_at[seq - 6])
                    w1_dma[seq] = wstream.dma(
                        w1r[:, (seq % 6) * 1024:(seq % 6 + 1) * 1024],
                        w1_src(kind, e, jp, ab))
                    w1_next[0] += 1

            def pump_w2(n=1):
                for _ in range(n):
                    seq = w2_next[0]
                    if seq >= len(w2_seq):
                        return
                    u, db = w2_seq[seq]
                    kind, e = units[u]
                    if seq >= 4:
                        S.wait("gpsimd", sem_t, w2_free_at[seq - 4])
                    w2_dma[seq] = wstream.dma(
                        w2r[:, (seq % 4) * 1024:(seq % 4 + 1) * 1024],
                        w2_src(kind, e, db))
                    w2_next[0] += 1

            pump_w1(6)
            pump_w2(4)
            psY4 = [psY[0], psY[1], psA[0], psA[1]]
            psY4_free = {0: 0, 1: 0, 2: 0, 3: 0}
            crow_map = {}

            def emit_extract_all():
                S.wait("tensor", sem_v, gating_done)
                crow_dma = {}
                for e in range(NE):
                    for g in range(4):
                        S.wait_cur("tensor", sem_s)
                        for q in range(4):
                            tt = g * 4 + q
                            S.op("tensor", "matmul",
                                 aux[0:1, 512 + q * 128:512 + (q + 1) * 128],
                                 comb_tok[:, tt * 15 + e:tt * 15 + e + 1],
                                 ident_sb[:], start=True, stop=True,
                                 inc=[(sem_t, 1)] if q == 3 else None)
                        S.wait_cur("scalar", sem_t)
                        if e > 0:
                            zstream.wait_for("scalar", crow_dma[e - 1])
                        S.op("scalar", "copy",
                             out=combrow[:, g * 512:(g + 1) * 512],
                             in_=aux[0:1, 512:1024], inc=[(sem_s, 1)])
                    S.wait_cur("sync", sem_s)
                    crow_dma[e] = zstream.dma(comb_T[e:e + 1, :], combrow[:])
                crow_map.update(crow_dma)
                return crow_dma[NE - 1]

            def emit_yinit(last_crow):
                zstream.wait_for("tensor", last_crow)
                yfree = {0: psY4_free[0], 1: psY4_free[1]}
                for db in range(8):
                    for c in range(4):
                        par = c % 2
                        S.wait("tensor", sem_v, yfree[par])
                        S.op("tensor", "matmul", psY[par][:],
                             fc2bT_sb[:, db * 128:(db + 1) * 128],
                             comb_T[0:15, c * 512:(c + 1) * 512],
                             start=True, stop=True, inc=[(sem_t, 1)])
                        S.wait_cur("vector", sem_t)
                        S.op("vector", "tensor_single_scalar",
                             ysb[:, db * TH + c * 512:db * TH + (c + 1) * 512],
                             psY[par][:], sfc2b_sb[:, db:db + 1], ALU.add,
                             inc=[(sem_v, 1)])
                        yfree[par] = S.mark(sem_v)
                psY4_free[0] = yfree[0]
                psY4_free[1] = yfree[1]

            def emit_comb(u):
                kind, e = units[u]
                slot = comb_b[u % 2]
                S.wait_cur("sync", sem_t)
                zstream.wait_for("sync", crow_map[e])
                ci = zstream.dma(combrow[:], comb_T[e:e + 1, :])
                zstream.wait_for("tensor", ci)
                S.wait_cur("scalar", sem_v)
                for g in range(4):
                    S.wait_cur("tensor", sem_s)
                    S.op("tensor", "matmul", aux[:, 0:512], ones_rbf[:],
                         combrow[:, g * 512:(g + 1) * 512], start=True,
                         stop=True, inc=[(sem_t, 1)])
                    S.wait_cur("scalar", sem_t)
                    S.op("scalar", "copy", out=slot[:, g * 512:(g + 1) * 512],
                         in_=aux[:, 0:512], inc=[(sem_s, 1)])
                return S.mark(sem_s)

            comb_ready = {}
            psA3 = [psA[0], psA[1], psY[0]]
            psB3 = [psB[0], psB[1], psY[1]]
            psAB_free = {0: (0, 0), 1: (0, 0), 2: (0, 0)}
            gact_free = {0: 0, 1: 0, 2: 0}
            y_prog = {}
            deferred = [0]

            for u, (kind, e) in enumerate(units):
                npar = 2 if u == 0 else 3
                it = 0
                for jp in range(8):
                    seqA = u * 16 + 2 * jp
                    seqB = seqA + 1
                    bA, bB, gmc = bias_cols(kind, e, jp)
                    for c in range(4):
                        par = it % npar
                        if c == 0:
                            wstream.wait_for("tensor", w1_dma[seqA])
                            wstream.wait_for("tensor", w1_dma[seqB])
                        sB, vA = psAB_free[par]
                        S.wait("tensor", sem_s, sB)
                        S.wait("tensor", sem_v, vA)
                        if it < npar:
                            if par < 2:
                                S.wait("tensor", sem_v, psY4_free[2 + par])
                            else:
                                S.wait("tensor", sem_v, psY4_free[0])
                                S.wait("tensor", sem_v, psY4_free[1])
                        it += 1
                        wA = w1r[:, (seqA % 6) * 1024:(seqA % 6 + 1) * 1024]
                        wB = w1r[:, (seqB % 6) * 1024:(seqB % 6 + 1) * 1024]
                        for d in range(8):
                            S.op("tensor", "matmul", psA3[par][:],
                                 wA[:, d * 128:(d + 1) * 128],
                                 xn[:, d * TH + c * 512:d * TH + (c + 1) * 512],
                                 start=(d == 0), stop=(d == 7))
                        for d in range(8):
                            S.op("tensor", "matmul", psB3[par][:],
                                 wB[:, d * 128:(d + 1) * 128],
                                 xn[:, d * TH + c * 512:d * TH + (c + 1) * 512],
                                 start=(d == 0), stop=(d == 7),
                                 inc=[(sem_t, 1)] if d == 7 else None)
                        tmark = S.mark(sem_t)
                        S.wait("scalar", sem_t, tmark)
                        S.wait("scalar", sem_v, gact_free[par])
                        S.op("scalar", "activation", out=g_act[par][:],
                             in_=psB3[par][:], func=AF.Gelu, bias=bB,
                             scale=1.0, inc=[(sem_s, 1)])
                        smark = S.mark(sem_s)
                        S.wait("vector", sem_s, smark)
                        S.op("vector", "tensor_scalar", tmp_a[par][:],
                             psA3[par][:], bA, gmc, ALU.add, ALU.mult)
                        S.op("vector", "tensor_tensor",
                             a_sb[:, jp * TH + c * 512:jp * TH + (c + 1) * 512],
                             tmp_a[par][:], g_act[par][:], ALU.mult,
                             inc=[(sem_v, 1)])
                        psAB_free[par] = (smark, S.mark(sem_v))
                        gact_free[par] = S.mark(sem_v)
                    w1_free_at[seqA] = S.mark(sem_t)
                    w1_free_at[seqB] = S.mark(sem_t)
                    pump_w1(2)
                    if u == 0 and jp == 1 and deferred[0] == 0:
                        deferred[0] = 1
                        last_crow = emit_extract_all()
                    if u == 0 and jp == 4 and deferred[0] == 1:
                        deferred[0] = 2
                        emit_yinit(last_crow)
                        comb_ready[2] = emit_comb(2)
                a_done = S.mark(sem_v)

                if kind == 0:
                    S.wait("vector", sem_s, comb_ready[u])
                    for jp in range(8):
                        S.op("vector", "tensor_tensor",
                             a_sb[:, jp * TH:(jp + 1) * TH],
                             a_sb[:, jp * TH:(jp + 1) * TH],
                             comb_b[u % 2][:], ALU.mult,
                             inc=[(sem_v, 1)] if jp == 7 else None)
                    a_done = S.mark(sem_v)
                if u + 1 < len(units) and units[u + 1][0] == 0:
                    comb_ready[u + 1] = emit_comb(u + 1)

                for db in range(8):
                    seq2 = u * 8 + db
                    wstream.wait_for("tensor", w2_dma[seq2])
                    if db == 0:
                        S.wait("tensor", sem_v, a_done)
                    w2a = w2r[:, (seq2 % 4) * 1024:(seq2 % 4 + 1) * 1024]
                    for c in range(4):
                        par = c
                        bank = psY4[par]
                        S.wait("tensor", sem_v, psY4_free[par])
                        for f in range(8):
                            S.op("tensor", "matmul", bank[:],
                                 w2a[:, f * 128:(f + 1) * 128],
                                 a_sb[:, f * TH + c * 512:f * TH + (c + 1) * 512],
                                 start=(f == 0), stop=(f == 7),
                                 inc=[(sem_t, 1)] if f == 7 else None)
                        S.wait_cur("vector", sem_t)
                        S.op("vector", "tensor_tensor",
                             ysb[:, db * TH + c * 512:db * TH + (c + 1) * 512],
                             ysb[:, db * TH + c * 512:db * TH + (c + 1) * 512],
                             bank[:], ALU.add, inc=[(sem_v, 1)])
                        psY4_free[par] = S.mark(sem_v)
                    w2_free_at[seq2] = S.mark(sem_t)
                    pump_w2(1)
                    y_prog[db] = S.mark(sem_v)

            st_base = n_stores[0]
            res_v = {}
            for d8 in range(8):
                par = d8 % 2
                rio = stg[:, par * 2048:(par + 1) * 2048]
                if d8 >= 2:
                    S.wait("sync", sem_v, res_v[d8 - 2])
                    S.wait("sync", sem_st, 16 * (st_base + d8 - 1))
                else:
                    S.wait("sync", sem_v, xn_ready)
                idx = zstream.dma(rio, z_in[d8 * 128:(d8 + 1) * 128,
                                            half * TH:(half + 1) * TH])
                zstream.wait_for("vector", idx)
                S.wait("vector", sem_v, y_prog[d8])
                S.op("vector", "scalar_tensor_tensor", rio,
                     ysb[:, d8 * TH:(d8 + 1) * TH], 0.5, rio,
                     ALU.mult, ALU.add, inc=[(sem_v, 1)])
                res_v[d8] = S.mark(sem_v)
                S.wait("sync", sem_v, res_v[d8])
                store(z_out[d8 * 128:(d8 + 1) * 128, half * TH:(half + 1) * TH],
                      rio)
        light_barrier()
        ff.close()

    def emit_attn(z_in, z_out, tag="at"):
        slopes = _slopes16()
        ff = contextlib.ExitStack()

        def fsb(name, shape, dtype):
            return ff.enter_context(nc.sbuf_tensor(f"{tag}{name}", list(shape), dtype))

        def fps(name, shape):
            return ff.enter_context(nc.psum_tensor(f"{tag}{name}", list(shape), F32))

        xn = fsb("xn", (128, 8 * T), BF16)
        osb = fsb("o", (128, 8 * T), BF16)
        vbuf = fsb("vb", (128, 32 * 260), BF16)
        qb = fsb("qb", (128, T), BF16)
        kb = fsb("kb", (128, T), BF16)
        pt = [fsb(f"pt{i}", (128, 512), BF16) for i in range(4)]
        invhl = fsb("invhl", (64, T), BF16)
        wqkv = fsb("wqkv", (128, 1024), BF16)
        wo_r = fsb("wo_r", (128, 2 * 1024), BF16)
        wvg = wo_r[:]
        stg = fsb("stg", (128, 4096), F32)
        bvb = fsb("bvb", (128, E), BF16)
        masks_sb = fsb("masks", (128, 4 * 512), BF16)
        mb = fsb("mb", (128, 512), F32)
        rstdb = fsb("rstdb", (128, 512), F32)
        tmpf = fsb("tmpf", (128, 512), F32)
        mrow = rstdb[0:1, 0:512]
        vrow = mb[0:1, 0:512]
        bvrow_l = stg[0:1, 0:E]

        psS = [fps(f"psS{i}", (128, 512)) for i in range(4)]
        psO = fps("psO", (128, 512))
        psQ = fps("psQ", (128, 512))
        psP = [fps(f"psP{i}", (128, 512)) for i in range(2)]

        class _Aux:
            def __getitem__(self, key):
                p, c = key
                if isinstance(c, slice) and c.start == 512:
                    return psS[1][p, c.start - 512:c.stop - 512]
                return psS[0][p, c]
        _aux = _Aux()

        lnB = dict(
            zbf=[osb[:, 0:4096], osb[:, 8192:12288]],
            zsq=[osb[:, 4096:8192], osb[:, 12288:16384]],
            psBc=[(psS[0][:], psS[1][:]), (psS[2][:], psS[3][:])],
            rows=[(rstdb[0:1, 0:512], mb[0:1, 0:512], tmpf[0:1, 0:512]),
                  (stg[0:1, 0:512], stg[0:1, 512:1024],
                   stg[0:1, 1024:1536])],
            lhs=[(ones_r32[0:1, :], ones_r32[0:1, :]),
                 (ones_r32[0:1, :], ones_r32[0:1, :])],
            tmp=[mb[:], rstdb[:]], mv={}, mt={})

        light_barrier()
        wstream.dma(masks_sb[:].rearrange("p (r q) -> p r q", r=4),
                    dt["masks"].rearrange("r p q -> p r q"))
        wstream.dma(bvrow_l[:], dt["bvrow"])
        wstream.wait_all("vector")
        wstream.wait_all("tensor")
        S.wait_cur("tensor", sem_v)
        S.op("tensor", "matmul", psS[0][:], ones_r32[0:1, :],
             bvrow_l[:, 0:512], start=True, stop=True)
        S.op("tensor", "matmul", psS[1][:], ones_r32[0:1, :],
             bvrow_l[:, 512:1024], start=True, stop=True,
             inc=[(sem_t, 1)])
        S.wait_cur("scalar", sem_t)
        S.op("scalar", "copy", out=bvb[:, 0:512], in_=psS[0][:])
        S.op("scalar", "copy", out=bvb[:, 512:1024], in_=psS[1][:],
             inc=[(sem_s, 1)])
        S.wait_cur("tensor", sem_s)

        psVp = [psP[1], psQ]
        psVp_hold = {0: {"v": 0, "s": 0}, 1: {"v": 0, "s": 0}}
        psO_free = {0: 0, 1: 0}
        pt_free = {0: 0, 1: 0, 2: 0, 3: 0}
        psS_war = {0: 0, 1: 0}
        pend_norm = [None]
        pend_epi = [None]
        last_S_mark = 0

        def flush_epi():
            if pend_epi[0] is None:
                return
            e_ps, e_ob, e_dt, e_pb, e_qc, e_pv = pend_epi[0]
            pend_epi[0] = None
            S.wait("vector", sem_t, e_pv)
            S.op("vector", "tensor_copy",
                 invhl[0:1, e_qc * 512:(e_qc + 1) * 512], e_ps[64:65, :])
            S.op("vector", "tensor_tensor",
                 invhl[32:33, e_qc * 512:(e_qc + 1) * 512],
                 e_ps[64:65, :],
                 invhl[0:1, e_qc * 512:(e_qc + 1) * 512],
                 ALU.subtract)
            S.op("vector", "tensor_copy",
                 osb[e_pb:e_pb + 64,
                     e_dt * T + e_qc * 512:e_dt * T + (e_qc + 1) * 512],
                 e_ps[0:64, :], inc=[(sem_v, 1)])
            psO_free[e_ob] = S.mark(sem_v)

        def emit_norm():
            if pend_norm[0] is None:
                return
            ndt, npb = pend_norm[0]
            pend_norm[0] = None
            S.wait_cur("tensor", sem_v)
            for nqc in range(8):
                par = nqc % 2
                bank = psS[2 + par]
                S.wait("tensor", sem_s, pt_free[2 + par])
                S.wait("tensor", sem_v, psS_war[par])
                S.op("tensor", "matmul", bank[0:64, :],
                     selq_sb[0:1, 0:64],
                     invhl[0:1, nqc * 512:(nqc + 1) * 512],
                     start=True, stop=True)
                S.op("tensor", "matmul", bank[64:128, :],
                     selq_sb[32:33, 0:64],
                     invhl[32:33, nqc * 512:(nqc + 1) * 512],
                     start=True, stop=True, inc=[(sem_t, 1)])
                nm = S.mark(sem_t)
                S.wait("vector", sem_t, nm)
                S.op("vector", "tensor_copy", tmpf[npb:npb + 64, :],
                     bank[64:128, :])
                S.op("vector", "tensor_tensor", tmpf[npb:npb + 64, :],
                     tmpf[npb:npb + 64, :], bank[0:64, :], ALU.add)
                S.op("vector", "reciprocal", tmpf[npb:npb + 64, :],
                     tmpf[npb:npb + 64, :])
                S.op("vector", "tensor_tensor",
                     osb[npb:npb + 64,
                         ndt * T + nqc * 512:ndt * T + (nqc + 1) * 512],
                     osb[npb:npb + 64,
                         ndt * T + nqc * 512:ndt * T + (nqc + 1) * 512],
                     tmpf[npb:npb + 64, :], ALU.mult,
                     inc=[(sem_v, 1)])
                psS_war[par] = S.mark(sem_v)
        S.op("vector", "memset", qb[64:70, :], 1.0)
        S.op("vector", "memset", kb[64:70, :], 1.0)
        S.op("vector", "memset", vbuf[:], 1.0, inc=[(sem_v, 1)])
        init_done = S.mark(sem_v)
        S.wait("gpsimd", sem_v, init_done)
        vw0_idx = wstream.dma(wvg[:], dt["wvs"][0].rearrange("p d j -> p (d j)"))
        wstream.wait_for("tensor", vw0_idx)
        for tc in range(8):
            def xn_dst(d, tc=tc):
                return xn[:, d * T + tc * 512:d * T + (tc + 1) * 512]
            emit_ln_chunk(z_in, tc, xn_dst, lnB)
            for tt in range(4 * tc, 4 * tc + 4):
                par = tt % 2
                pb = psVp[par]
                S.wait("tensor", sem_v, lnB["mv"][tc])
                S.wait("tensor", sem_v, psVp_hold[par]["v"])
                S.wait("tensor", sem_s, psVp_hold[par]["s"])
                for d in range(8):
                    S.op("tensor", "matmul", pb[:, 0:256],
                         xn[:, d * T + tt * 128:d * T + (tt + 1) * 128],
                         wvg[:, d * 256:(d + 1) * 256],
                         start=(d == 0), stop=(d == 7),
                         inc=[(sem_t, 1)] if d == 7 else None)
                S.wait_cur("vector", sem_t)
                dst = vbuf[:, tt * 260:(tt + 1) * 260].rearrange(
                    "p (hh c) -> p hh c", hh=4)[:, :, 0:64]
                S.op("vector", "tensor_tensor", dst,
                     pb[:, 0:256].rearrange("p (hh c) -> p hh c", hh=4),
                     bvb[:, 0:256].rearrange("p (hh c) -> p hh c", hh=4),
                     ALU.add, inc=[(sem_v, 1)])
                psVp_hold[par]["v"] = S.mark(sem_v)
        xn_ready = S.mark(sem_v)

        def kt_min(h, qc):
            d = int(WIN_C / slopes[h])
            return max(0, (qc * 512 - d - 127) // 128)

        for g in range(4):
            if g > 0:
                vw_idx = wstream.dma(wvg[:],
                                     dt["wvs"][g].rearrange("p d j -> p (d j)"))
                wstream.wait_for("tensor", vw_idx)
                for tt in range(32):
                    par = tt % 2
                    pb = psVp[par]
                    S.wait("tensor", sem_v, psVp_hold[par]["v"])
                    S.wait("tensor", sem_s, psVp_hold[par]["s"])
                    for d in range(8):
                        S.op("tensor", "matmul", pb[:, 0:256],
                             xn[:, d * T + tt * 128:d * T + (tt + 1) * 128],
                             wvg[:, d * 256:(d + 1) * 256],
                             start=(d == 0), stop=(d == 7),
                             inc=[(sem_t, 1)] if d == 7 else None)
                    S.wait_cur("vector", sem_t)
                    dst = vbuf[:, tt * 260:(tt + 1) * 260].rearrange(
                        "p (hh c) -> p hh c", hh=4)[:, :, 0:64]
                    S.op("vector", "tensor_tensor", dst,
                         pb[:, 0:256].rearrange("p (hh c) -> p hh c", hh=4),
                         bvb[:, g * 256:(g + 1) * 256].rearrange(
                             "p (hh c) -> p hh c", hh=4),
                         ALU.add, inc=[(sem_v, 1)])
                    psVp_hold[par]["v"] = S.mark(sem_v)

            for h in range(4 * g, 4 * g + 4):
                dt_h = h // 2
                pbase = (h % 2) * 64
                S.wait("gpsimd", sem_t, last_S_mark)
                qk_idx = wstream.dma(wqkv[:],
                                     dt["wqks"][h].rearrange("p d j -> p (d j)"))
                aq_idx = wstream.dma(qb[64:67, :], dt["alq"][h])
                ak_idx = wstream.dma(kb[67:70, :], dt["alk"][h])
                wstream.wait_for("tensor", qk_idx)
                for cc in range(8):
                    par = cc % 2
                    pb = psVp[par]
                    S.wait("tensor", sem_s, psVp_hold[par]["s"])
                    S.wait("tensor", sem_v, psVp_hold[par]["v"])
                    for d in range(8):
                        S.op("tensor", "matmul", pb[:],
                             wqkv[:, d * 128:(d + 1) * 128],
                             xn[:, d * T + cc * 512:d * T + (cc + 1) * 512],
                             start=(d == 0), stop=(d == 7),
                             inc=[(sem_t, 1)] if d == 7 else None)
                    pj_mm = S.mark(sem_t)
                    S.wait("scalar", sem_t, pj_mm)
                    S.op("scalar", "activation",
                         out=qb[0:64, cc * 512:(cc + 1) * 512],
                         in_=pb[0:64, :], func=AF.Identity,
                         bias=bqk_sb[0:64, h:h + 1], scale=1.0)
                    S.op("scalar", "activation",
                         out=kb[0:64, cc * 512:(cc + 1) * 512],
                         in_=pb[64:128, :], func=AF.Identity,
                         bias=bqk_sb[64:128, h:h + 1],
                         scale=1.0, inc=[(sem_s, 1)])
                    psVp_hold[par]["s"] = S.mark(sem_s)
                proj_done = S.mark(sem_s)
                wstream.wait_for("tensor", aq_idx)
                wstream.wait_for("tensor", ak_idx)
                emit_norm()
                S.wait("tensor", sem_s, proj_done)

                for qc in range(8):
                    kths = list(range(kt_min(h, qc), 4 * qc + 4))
                    nk = len(kths)
                    ob = qc % 2
                    psOb = psO if ob == 0 else psP[0]
                    emarks = {}
                    LA = 3
                    for idx in range(nk + LA):
                        if idx == 2:
                            flush_epi()
                        if idx < nk:
                            kt = kths[idx]
                            par = idx % 4
                            S.wait("tensor", sem_s, pt_free[par])
                            if par >= 2:
                                S.wait("tensor", sem_v, psS_war[par - 2])
                            S.op("tensor", "matmul", psS[par][:],
                                 kb[0:70, kt * 128:(kt + 1) * 128],
                                 qb[0:70, qc * 512:(qc + 1) * 512],
                                 start=True, stop=True, inc=[(sem_t, 1)])
                            s_mm = S.mark(sem_t)
                            last_S_mark = s_mm
                            r = kt * 128 - qc * 512
                            if r >= 0:
                                ri = r // 128
                                S.wait("vector", sem_t, s_mm)
                                S.op("vector", "tensor_tensor", psS[par][:],
                                     psS[par][:],
                                     masks_sb[:, ri * 512:(ri + 1) * 512],
                                     ALU.add, inc=[(sem_v, 1)])
                                S.wait_cur("scalar", sem_v)
                            else:
                                S.wait("scalar", sem_t, s_mm)
                            S.op("scalar", "activation", out=pt[par][:],
                                 in_=psS[par][:], func=AF.Exp,
                                 inc=[(sem_s, 1)])
                            emarks[idx] = S.mark(sem_s)
                        j = idx - LA
                        if j >= 0:
                            kt_j = kths[j]
                            par_j = j % 4
                            S.wait("tensor", sem_s, emarks[j])
                            if j == 0:
                                S.wait("tensor", sem_v, psO_free[ob])
                            S.op("tensor", "matmul", psOb[0:65, :],
                                 vbuf[:, kt_j * 260 + (h % 4) * 65:
                                      kt_j * 260 + (h % 4) * 65 + 65],
                                 pt[par_j][:], start=(j == 0),
                                 stop=(j == nk - 1),
                                 inc=[(sem_t, 1)] if j == nk - 1 else None)
                            pt_free[par_j] = emarks[j]
                    pend_epi[0] = (psOb, ob, dt_h, pbase, qc,
                                   S.mark(sem_t))
                flush_epi()
                pend_norm[0] = (dt_h, pbase)
        emit_norm()
        o_done = S.mark(sem_v)

        S.wait("tensor", sem_v, o_done)
        S.wait_cur("gpsimd", sem_t)
        wo_dma = {}
        wo_seq = 0
        wo_free = {}

        def pump_wo(jt_seq):
            nonlocal wo_seq
            if jt_seq >= 2:
                S.wait("gpsimd", sem_t, wo_free[jt_seq - 2])
            wo_dma[jt_seq] = wstream.dma(
                wo_r[:, (jt_seq % 2) * 1024:(jt_seq % 2 + 1) * 1024],
                dt["wos"][jt_seq % 8].rearrange("p d j -> p (d j)"))

        pump_wo(0)
        pump_wo(1)
        psW_free = {0: 0, 1: 0}
        ring = [mb[:], rstdb[:], tmpf[:]]
        ring_v = {}
        ld_idx = {}
        st_of = {}

        def emit_rload(k):
            tcx, jtx = divmod(k, 8)
            if k >= 3:
                S.wait("sync", sem_v, ring_v[k - 3])
            else:
                S.wait("sync", sem_v, o_done)
            ld_idx[k] = zstream.dma(
                ring[k % 3], z_in[jtx * 128:(jtx + 1) * 128,
                                  tcx * 512:(tcx + 1) * 512])

        for k in range(3):
            emit_rload(k)
        for tc in range(8):
            for jt in range(8):
                jt_seq = tc * 8 + jt
                wstream.wait_for("tensor", wo_dma[jt_seq])
                par = jt % 2
                woa = wo_r[:, (jt_seq % 2) * 1024:(jt_seq % 2 + 1) * 1024]
                S.wait("tensor", sem_v, psW_free[par])
                for d in range(8):
                    S.op("tensor", "matmul", psS[par][:],
                         woa[:, d * 128:(d + 1) * 128],
                         osb[:, d * T + tc * 512:d * T + (tc + 1) * 512],
                         start=(d == 0), stop=(d == 7),
                         inc=[(sem_t, 1)] if d == 7 else None)
                wo_free[jt_seq] = S.mark(sem_t)
                if jt_seq + 2 < 64:
                    pump_wo(jt_seq + 2)
                zstream.wait_for("vector", ld_idx[jt_seq])
                if tc > 0:
                    S.wait("vector", sem_st,
                           16 * (st_of[jt_seq - 8] + 1))
                S.wait("vector", sem_t, wo_free[jt_seq])
                S.op("vector", "scalar_tensor_tensor",
                     stg[:, jt * 512:(jt + 1) * 512], psS[par][:],
                     bo_sb[:, jt:jt + 1], ring[jt_seq % 3],
                     ALU.add, ALU.add, inc=[(sem_v, 1)])
                ring_v[jt_seq] = S.mark(sem_v)
                psW_free[par] = ring_v[jt_seq]
                S.wait("sync", sem_v, ring_v[jt_seq])
                st_of[jt_seq] = n_stores[0]
                store(z_out[jt * 128:(jt + 1) * 128,
                            tc * 512:(tc + 1) * 512],
                      stg[:, jt * 512:(jt + 1) * 512])
                if jt_seq + 3 < 64:
                    emit_rload(jt_seq + 3)
        light_barrier()
        ff.close()

    block = es.enter_context(nc.Block())
    for p in parts:
        z_in, z_out = io[p]
        if p in ("ffn1", "ffn2"):
            emit_ffn(z_in, z_out, "f1" if p == "ffn1" else "f2")
        else:
            emit_attn(z_in, z_out)
    S.wait("sync", sem_st, 16 * n_stores[0])
    S.replay(block)
    es.close()
    return nc


def _host_shared(i):
    f32 = np.float32
    out = {}
    fc1_w = np.asarray(i["L0_ffn_fc1_w"], f32)
    out["w1s"] = np.ascontiguousarray(
        fc1_w.reshape(NE, 16, 128, 8, 128).transpose(0, 1, 4, 3, 2))
    fc2_w = np.asarray(i["L0_ffn_fc2_w"], f32)
    out["w2s"] = np.ascontiguousarray(
        fc2_w.reshape(NE, 8, 128, 8, 128).transpose(0, 1, 4, 3, 2))
    s1 = np.asarray(i["L0_ffn_s_fc1_w"], f32)
    out["ws1s"] = np.ascontiguousarray(
        s1.reshape(32, 128, 8, 128).transpose(0, 3, 2, 1))
    s2 = np.asarray(i["L0_ffn_s_fc2_w"], f32)
    out["ws2s"] = np.ascontiguousarray(
        s2.reshape(8, 128, 16, 128).transpose(0, 3, 2, 1))
    gw = np.asarray(i["L0_ffn_gate_w"], f32)
    out["gws"] = np.ascontiguousarray(gw.reshape(15, 8, 128).transpose(2, 1, 0))
    wq8 = np.asarray(i["L0_attn_Wq"], f32) * 0.125
    wk = np.asarray(i["L0_attn_Wk"], f32)
    qpart = wq8.reshape(16, 64, 8, 128).transpose(0, 3, 2, 1)
    kpart = wk.reshape(16, 64, 8, 128).transpose(0, 3, 2, 1)
    out["wqks"] = np.ascontiguousarray(
        np.concatenate([qpart, kpart], axis=3))
    wv = np.asarray(i["L0_attn_Wv"], f32)
    out["wvs"] = np.ascontiguousarray(
        wv.reshape(4, 256, 8, 128).transpose(0, 3, 2, 1))
    wo = np.asarray(i["L0_attn_Wo"], f32)
    out["wos"] = np.ascontiguousarray(
        wo.reshape(8, 128, 8, 128).transpose(0, 3, 2, 1))
    out["fc1b"] = np.ascontiguousarray(
        np.asarray(i["L0_ffn_fc1_b"], f32).reshape(NE, 16, 128)
        .transpose(2, 0, 1).reshape(128, NE * 16))
    out["fc2bT"] = np.ascontiguousarray(np.asarray(i["L0_ffn_fc2_b"], f32))
    out["gmult"] = np.ascontiguousarray(
        np.asarray(i["L0_ffn_geglu_mult"], f32).reshape(NE, 8, 128)
        .transpose(2, 0, 1).reshape(128, NE * 8))
    out["sfc1b"] = np.ascontiguousarray(
        np.asarray(i["L0_ffn_s_fc1_b"], f32).reshape(32, 128).T)
    out["sgm"] = np.ascontiguousarray(
        np.asarray(i["L0_ffn_s_geglu_mult"], f32).reshape(16, 128).T)
    out["sfc2b"] = np.ascontiguousarray(
        np.asarray(i["L0_ffn_s_fc2_b"], f32).reshape(8, 128).T)
    bq8 = np.asarray(i["L0_attn_bq"], f32).reshape(16, 64) * 0.125
    bk_ = np.asarray(i["L0_attn_bk"], f32).reshape(16, 64)
    out["bqkv"] = np.ascontiguousarray(
        np.concatenate([bq8, bk_], axis=1).T)
    selq = np.zeros((128, 128), f32)
    selq[0, 0:64] = 1.0
    selq[32, 0:64] = 1.0
    out["selqv"] = selq
    out["bvrow"] = np.asarray(i["L0_attn_bv"], f32).reshape(1, E)
    out["bov"] = np.ascontiguousarray(np.asarray(i["L0_attn_bo"], f32).reshape(8, 128).T)

    slopes = _slopes16().astype(np.float64)
    pos = np.arange(T, dtype=np.float64)
    alq = np.zeros((H, 3, T), np.float32)
    alk = np.zeros((H, 3, T), np.float32)
    for h in range(H):
        alq[h] = _tri_split(-slopes[h] * pos)
        alk[h] = _tri_split(slopes[h] * pos)
    out["alq"] = alq
    out["alk"] = alk
    masks = np.zeros((4, 128, 512), np.float32)
    kk = np.arange(128)[:, None]
    qq = np.arange(512)[None, :]
    for r in range(4):
        masks[r] = np.where(kk + r * 128 <= qq, 0.0, NEG)
    out["masks"] = masks
    out["identb"] = np.eye(128, dtype=np.float32)
    sent = np.array([-(1e30 * (1.0 + 0.01 * k)) for k in range(1, 9)], np.float32)
    out["sentv"] = np.tile(sent, (128, 1))
    return out


_CACHED = {}


def kernel(**inputs):
    parts = _CACHED.get("parts", ("ffn1", "attn", "ffn2"))
    key = ("nc", parts)
    if key not in _CACHED:
        _CACHED[key] = build_program(parts)
    nc = _CACHED[key]
    shared = _host_shared(inputs)
    x = np.asarray(inputs["L0_attn_x"], np.float32)
    in_maps = []
    for b in range(8):
        m = dict(shared)
        m["xT"] = np.ascontiguousarray(x[b].T)
        in_maps.append(m)
    res = run_bass_kernel_spmd(nc, in_maps, core_ids=list(range(8)))
    out = np.stack([np.asarray(r["outT"]).T for r in res.results])
    return np.ascontiguousarray(out.astype(np.float32))


# version 20
# speedup vs baseline: 1.0766x; 1.0766x over previous
import contextlib
import math

import numpy as np
import ml_dtypes
import concourse.bass as bass
import concourse.mybir as mybir
from concourse.bass_utils import run_bass_kernel_spmd

F32 = mybir.dt.float32
BF16 = mybir.dt.bfloat16
AF = mybir.ActivationFunctionType
ALU = mybir.AluOpType
AX = mybir.AxisListType

E = 1024
T = 4096
H = 16
DH = 64
NE = 15
TH = 2048
NEG = -1.0e30
WIN_C = 60.0

ENGS = ("tensor", "vector", "scalar", "sync", "gpsimd")


def _slopes16():
    start = 2.0 ** (-2.0 ** (-(math.log2(16) - 3)))
    return np.array([start * start ** i for i in range(16)], dtype=np.float32)


def _tri_split(v):
    hi = v.astype(ml_dtypes.bfloat16).astype(np.float64)
    r = v - hi
    mid = r.astype(ml_dtypes.bfloat16).astype(np.float64)
    lo = (r - mid).astype(ml_dtypes.bfloat16).astype(np.float64)
    return np.stack([hi, mid, lo]).astype(np.float32)


class Sched:
    def __init__(self, nc):
        self.nc = nc
        self.ops = {e: [] for e in ENGS}
        self.cnt = {}

    def sem(self, handle):
        self.cnt[handle.name] = 0
        return handle

    def op(self, eng, method, *args, inc=None, **kw):
        incs = []
        if inc:
            for s, n in inc:
                self.cnt[s.name] += n
                incs.append((s, n))
        self.ops[eng].append(("op", method, args, kw, incs))

    def mark(self, sem):
        return self.cnt[sem.name]

    def wait(self, eng, sem, val):
        if val > 0:
            self.ops[eng].append(("wait", sem, val))

    def wait_cur(self, eng, sem):
        self.wait(eng, sem, self.cnt[sem.name])

    def replay(self, block):
        for eng in ENGS:
            lst = self.ops[eng]

            def body(handle, lst=lst):
                for d in lst:
                    if d[0] == "op":
                        _, method, args, kw, incs = d
                        ins = getattr(handle, method)(*args, **kw)
                        for s, n in incs:
                            ins.then_inc(s, n)
                    else:
                        handle.wait_ge(d[1], d[2])
            getattr(block, eng)(body)


class DmaStream:
    def __init__(self, sched, eng, sems):
        self.s, self.eng, self.sems, self.n = sched, eng, sems, 0

    def dma(self, out, in_, **kw):
        sem = self.sems[self.n % len(self.sems)]
        self.s.op(self.eng, "dma_start", inc=[(sem, 16)], out=out, in_=in_, **kw)
        self.n += 1
        return self.n - 1

    def wait_for(self, eng, idx):
        k = len(self.sems)
        self.s.wait(eng, self.sems[idx % k], 16 * (idx // k + 1))

    def wait_all(self, eng):
        for i, sem in enumerate(self.sems):
            tot = (self.n - 1 - i) // len(self.sems) + 1 if self.n > i else 0
            if tot:
                self.s.wait(eng, sem, 16 * tot)


def build_program(parts=("ffn1", "attn", "ffn2")):
    nc = bass.Bass("TRN2", target_bir_lowering=False, debug=False)
    dt = {}

    def din(name, shape):
        dt[name] = nc.dram_tensor(name, list(shape), F32, kind="ExternalInput").ap()

    din("xT", (E, T))
    din("w1s", (NE, 16, 128, 8, 128))
    din("w2s", (NE, 8, 128, 8, 128))
    din("ws1s", (32, 128, 8, 128))
    din("ws2s", (8, 128, 16, 128))
    din("gws", (128, 8, 15))
    din("wqks", (H, 128, 8, 128))
    din("wvs", (4, 128, 8, 4 * DH))
    din("wos", (8, 128, 8, 128))
    din("fc1b", (128, NE * 16))
    din("fc2bT", (NE, E))
    din("gmult", (128, NE * 8))
    din("sfc1b", (128, 32))
    din("sgm", (128, 16))
    din("sfc2b", (128, 8))
    din("bqkv", (128, H))
    din("selqv", (128, 128))
    din("bvrow", (1, E))
    din("bov", (128, 8))
    din("alq", (H, 3, T))
    din("alk", (H, 3, T))
    din("masks", (4, 128, 512))
    din("identb", (128, 128))
    din("sentv", (128, 8))
    outT = nc.dram_tensor("outT", [E, T], F32, kind="ExternalOutput").ap()
    z1T = nc.dram_tensor("z1T", [E, T], F32, kind="Internal").ap()
    z2T = nc.dram_tensor("z2T", [E, T], F32, kind="Internal").ap()

    chain = [dt["xT"], z1T, z2T]
    io = {}
    for i, p in enumerate(parts):
        io[p] = (chain[i], outT if i == len(parts) - 1 else chain[i + 1])

    es = contextlib.ExitStack()

    def sb(name, shape, dtype):
        return es.enter_context(nc.sbuf_tensor(name, list(shape), dtype))

    bqk_sb = sb("bqk_sb", (128, H), F32)
    selq_sb = sb("selq_sb", (128, 128), BF16)
    bo_sb = sb("bo_sb", (128, 8), F32)
    ones_r32 = sb("ones_r32", (65, 128), F32)
    ones_rbf = sb("ones_rbf", (1, 128), BF16)
    ones_cbf = sb("ones_cbf", (128, 1), BF16)
    eps_sb = sb("eps_sb", (1, 1), F32)

    S = Sched(nc)

    def mksem(name):
        return S.sem(es.enter_context(nc.semaphore(name)))

    sem_t = mksem("sem_t")
    sem_v = mksem("sem_v")
    sem_s = mksem("sem_s")
    sem_w = [mksem(f"sem_w{i}") for i in range(4)]
    sem_zl = [mksem(f"sem_zl{i}") for i in range(2)]
    sem_st = mksem("sem_st")
    wstream = DmaStream(S, "gpsimd", sem_w)
    zstream = DmaStream(S, "sync", sem_zl)
    n_stores = [0]

    def store(out, in_):
        S.op("sync", "dma_start", inc=[(sem_st, 16)], out=out, in_=in_)
        n_stores[0] += 1

    def light_barrier():
        for e in ENGS:
            for sem in (sem_t, sem_v, sem_s):
                S.wait_cur(e, sem)
            wstream.wait_all(e)
            zstream.wait_all(e)
            S.wait(e, sem_st, 16 * n_stores[0])

    for dst, src in [(bqk_sb, "bqkv"), (selq_sb, "selqv"),
                     (bo_sb, "bov")]:
        wstream.dma(dst[:], dt[src])
    wstream.wait_all("vector")
    S.op("vector", "memset", ones_r32[:], 1.0)
    S.op("vector", "memset", ones_rbf[:], 1.0)
    S.op("vector", "memset", ones_cbf[:], 1.0)
    S.op("vector", "memset", eps_sb[:], 1e-5, inc=[(sem_v, 1)])
    for e in ("tensor", "scalar", "sync", "gpsimd"):
        wstream.wait_all(e)
        S.wait_cur(e, sem_v)

    def emit_ln_chunk(z_in, tc, xn_dst, B):
        p = tc % 2
        zbf, zsq = B["zbf"][p], B["zsq"][p]
        bc0, bc1 = B["psBc"][p]
        st1, st2 = bc0[0:1, 0:512], bc1[0:1, 0:512]
        mrow, vrow, m2row = B["rows"][p]
        tmp = B["tmp"][p]
        mv, mt = B["mv"], B["mt"]
        src = z_in.rearrange("(dt p) t -> p dt t", p=128)[:, :,
                                                          tc * 512:(tc + 1) * 512]
        stgL = B.get("stgL")
        if stgL is not None:
            sl = stgL[p]
            if tc >= 2:
                S.wait("sync", sem_v, mv[tc - 2])
            S.wait("sync", sem_st, 16 * n_stores[0])
            idx = zstream.dma(sl.rearrange("p (dt t) -> p dt t", dt=8), src)
            zstream.wait_for("scalar", idx)
            if tc >= 2:
                S.wait("scalar", sem_t, mt[tc - 2])
            S.op("scalar", "copy", out=zbf[:], in_=sl[:])
            S.op("scalar", "activation", out=zsq[:], in_=sl[:],
                 func=AF.Square, inc=[(sem_s, 1)])
            sub_src = sl
        else:
            if tc >= 2:
                S.wait("gpsimd", sem_v, mv[tc - 2])
            idx = wstream.dma(zbf.rearrange("p (dt t) -> p dt t", dt=8), src)
            wstream.wait_for("scalar", idx)
            if tc >= 2:
                S.wait("scalar", sem_t, mt[tc - 2])
            S.op("scalar", "activation", out=zsq[:], in_=zbf[:],
                 func=AF.Square, inc=[(sem_s, 1)])
            sub_src = zbf
        sq_m = S.mark(sem_s)
        S.wait("tensor", sem_s, sq_m)
        if tc >= 2:
            S.wait("tensor", sem_v, mv[tc - 2])
        for d in range(8):
            S.op("tensor", "matmul", st1, ones_cbf[:],
                 zbf[:, d * 512:(d + 1) * 512], start=(d == 0), stop=(d == 7))
        for d in range(8):
            S.op("tensor", "matmul", st2, ones_cbf[:],
                 zsq[:, d * 512:(d + 1) * 512], start=(d == 0), stop=(d == 7),
                 inc=[(sem_t, 1)] if d == 7 else None)
        mt[tc] = S.mark(sem_t)
        S.wait("vector", sem_t, mt[tc])
        S.op("vector", "tensor_scalar_mul", mrow, st1, 1.0 / 1024.0)
        S.op("vector", "tensor_tensor", m2row, mrow, mrow, ALU.mult)
        S.op("vector", "scalar_tensor_tensor", vrow, st2, 1.0 / 1024.0,
             m2row, ALU.mult, ALU.subtract, inc=[(sem_v, 1)])
        row_m = S.mark(sem_v)
        S.wait("scalar", sem_v, row_m)
        S.op("scalar", "activation", out=vrow, in_=vrow,
             func=AF.Ln, bias=eps_sb[:], scale=1.0)
        S.op("scalar", "activation", out=vrow, in_=vrow,
             func=AF.Exp, scale=-0.5, inc=[(sem_s, 1)])
        ex_m = S.mark(sem_s)
        lm, lv = B["lhs"][p]
        S.wait("tensor", sem_s, ex_m)
        S.op("tensor", "matmul", bc0[:, 0:512], lm, mrow,
             start=True, stop=True)
        S.op("tensor", "matmul", bc1[:, 0:512], lv, vrow,
             start=True, stop=True, inc=[(sem_t, 1)])
        bc_m = S.mark(sem_t)
        S.wait("vector", sem_t, bc_m)
        for d in range(8):
            S.op("vector", "tensor_tensor", tmp[:],
                 sub_src[:, d * 512:(d + 1) * 512], bc0[:, 0:512],
                 ALU.subtract)
            S.op("vector", "tensor_tensor", xn_dst(d), tmp[:],
                 bc1[:, 0:512], ALU.mult,
                 inc=[(sem_v, 1)] if d == 7 else None)
        mv[tc] = S.mark(sem_v)

    def emit_ffn(z_in, z_out, tag):
        ff = contextlib.ExitStack()

        def fsb(name, shape, dtype):
            return ff.enter_context(nc.sbuf_tensor(f"{tag}{name}", list(shape), dtype))

        def fps(name, shape):
            return ff.enter_context(nc.psum_tensor(f"{tag}{name}", list(shape), F32))

        xn = fsb("xn", (128, 8 * TH), BF16)
        ysb = fsb("y", (128, 8 * TH), F32)
        a_sb = fsb("a", (128, 8 * TH), BF16)
        stg = fsb("stg", (128, 4096), F32)
        fc1b_sb = fsb("fc1b_sb", (128, NE * 16), F32)
        gm_sb = fsb("gm_sb", (128, NE * 8), F32)
        sfc1b_sb = fsb("sfc1b_sb", (128, 32), F32)
        sgm_sb = fsb("sgm_sb", (128, 16), F32)
        sfc2b_sb = fsb("sfc2b_sb", (128, 8), F32)
        gate_sb = fsb("gate_sb", (128, 8 * 15), BF16)
        ident_sb = fsb("ident_sb", (128, 128), BF16)
        sent_sb = fsb("sent_sb", (128, 8), F32)
        w1r = fsb("w1r", (128, 6 * 1024), BF16)
        w2r = fsb("w2r", (128, 4 * 1024), BF16)
        comb_b = [fsb(f"combb{i}", (128, TH), BF16) for i in range(2)]
        combrow = fsb("combrow", (1, TH), BF16)
        comb_T = fsb("comb_T", (16, TH), BF16)
        fc2bT_sb = fsb("fc2bT_sb", (NE, E), BF16)
        comb_tok = fsb("comb_tok", (128, 16 * 15), BF16)
        mb = fsb("mb", (128, 512), F32)
        rstdb = fsb("rstdb", (128, 512), F32)
        tmpf = fsb("tmpf", (128, 512), F32)
        g_act = [fsb(f"gact{i}", (128, 512), BF16) for i in range(3)]
        tmp_a = [fsb(f"tmpa{i}", (128, 512), BF16) for i in range(3)]
        lgt_all = fsb("lgt_all", (128, 240), F32)
        work = fsb("work", (128, 23), F32)
        wout = fsb("wout", (128, 23), F32)
        m8 = fsb("m8", (128, 8), F32)
        rep = fsb("rep", (128, 8), F32)
        sel = fsb("sel", (128, 15), F32)
        exm = fsb("exm", (128, 15), F32)
        negm = fsb("negm", (128, 1), F32)
        gsum = fsb("gsum", (128, 1), F32)
        ginv = fsb("ginv", (128, 1), F32)

        psA = [fps(f"psA{i}", (128, 512)) for i in range(2)]
        psB = [fps(f"psB{i}", (128, 512)) for i in range(2)]
        psY = [fps(f"psY{i}", (128, 512)) for i in range(2)]
        aux = fps("aux", (128, 1024))

        lnB = dict(
            stgL=[stg[:, 0:4096], ysb[:, 0:4096]],
            zbf=[a_sb[:, 0:4096], a_sb[:, 8192:12288]],
            zsq=[a_sb[:, 4096:8192], a_sb[:, 12288:16384]],
            psBc=[(aux[:, 0:512], aux[:, 512:1024]),
                  (psA[0][:], psA[1][:])],
            rows=[(rstdb[0:1, 0:512], mb[0:1, 0:512], tmpf[0:1, 0:512]),
                  (tmpf[64:65, 0:512], tmpf[32:33, 0:512],
                   tmpf[96:97, 0:512])],
            lhs=[(ones_r32[0:1, :], ones_r32[0:1, :]),
                 (ones_r32[64:65, :], ones_r32[32:33, :])],
            tmp=[mb[:], rstdb[:]], mv={}, mt={})

        units = [(1, 0), (2, 0)] + [(0, e) for e in range(NE)]

        def w1_src(kind, e, jp, ab):
            if kind == 0:
                return dt["w1s"][e, jp + 8 * ab].rearrange("p d f -> p (d f)")
            j = (jp + 8 * (kind - 1)) + 16 * ab
            return dt["ws1s"][j].rearrange("p d f -> p (d f)")

        def w2_src(kind, e, db):
            if kind == 0:
                return dt["w2s"][e, db].rearrange("p f d -> p (f d)")
            f0 = 8 * (kind - 1)
            return dt["ws2s"][db, :, f0:f0 + 8, :].rearrange("p f d -> p (f d)")

        def bias_cols(kind, e, jp):
            if kind == 0:
                return (fc1b_sb[:, e * 16 + jp:e * 16 + jp + 1],
                        fc1b_sb[:, e * 16 + 8 + jp:e * 16 + 9 + jp],
                        gm_sb[:, e * 8 + jp:e * 8 + jp + 1])
            base = 8 * (kind - 1)
            return (sfc1b_sb[:, base + jp:base + jp + 1],
                    sfc1b_sb[:, 16 + base + jp:17 + base + jp],
                    sgm_sb[:, base + jp:base + jp + 1])

        for dst, src in [(fc1b_sb, "fc1b"), (gm_sb, "gmult"),
                         (sfc1b_sb, "sfc1b"), (sgm_sb, "sgm"),
                         (sfc2b_sb, "sfc2b"), (gate_sb, "gws"),
                         (ident_sb, "identb"), (sent_sb, "sentv")]:
            ap = dt[src]
            if src == "gws":
                ap = ap.rearrange("p d e -> p (d e)")
            wstream.dma(dst[:], ap)
        for e in ("tensor", "vector", "scalar"):
            wstream.wait_all(e)
        fc2bT_dma = wstream.dma(fc2bT_sb[:], dt["fc2bT"])
        wstream.wait_for("tensor", fc2bT_dma)
        for half in range(2):
            light_barrier()
            comb_state = {"mm": 0, "copy": 0, "mult": {0: 0, 1: 0}}

            for c in range(4):
                tc = half * 4 + c

                def xn_dst(d, c=c):
                    return xn[:, d * TH + c * 512:d * TH + (c + 1) * 512]
                emit_ln_chunk(z_in, tc, xn_dst, lnB)
            xn_ready = S.mark(sem_v)
            S.op("vector", "tensor_copy", work[:, 15:23], sent_sb[:],
                 inc=[(sem_v, 1)])

            S.wait("tensor", sem_v, xn_ready)
            for tt in range(16):
                for d in range(8):
                    S.op("tensor", "matmul", aux[:, tt * 15:(tt + 1) * 15],
                         xn[:, d * TH + tt * 128:d * TH + (tt + 1) * 128],
                         gate_sb[:, d * 15:(d + 1) * 15],
                         start=(d == 0), stop=(d == 7),
                         inc=[(sem_t, 1)] if (tt == 15 and d == 7) else None)
            S.wait_cur("scalar", sem_t)
            S.op("scalar", "copy", out=lgt_all[:], in_=aux[:, 0:240],
                 inc=[(sem_s, 1)])
            comb_state["copy"] = S.mark(sem_s)
            S.wait_cur("vector", sem_s)
            for tt in range(16):
                lgt = lgt_all[:, tt * 15:(tt + 1) * 15]
                S.op("vector", "tensor_copy", work[:, 0:15], lgt,
                     inc=[(sem_v, 1)])
                S.wait_cur("vector", sem_v)
                S.op("vector", "max", out=m8[:], in_=work[:], inc=[(sem_v, 1)])
                S.wait_cur("vector", sem_v)
                S.op("vector", "tensor_copy", rep[:, 0:3], m8[:, 0:3])
                S.op("vector", "tensor_copy", rep[:, 3:8], sent_sb[:, 0:5],
                     inc=[(sem_v, 1)])
                S.wait_cur("vector", sem_v)
                S.op("vector", "match_replace", out=wout[:], in_to_replace=rep[:],
                     in_values=work[:], imm_value=NEG, inc=[(sem_v, 1)])
                S.wait_cur("vector", sem_v)
                S.op("vector", "tensor_single_scalar", sel[:], wout[:, 0:15],
                     -1e29, ALU.is_lt)
                S.op("vector", "tensor_scalar_mul", negm[:], m8[:, 0:1], -1.0,
                     inc=[(sem_v, 1)])
                S.wait_cur("scalar", sem_v)
                S.op("scalar", "activation", out=exm[:], in_=lgt,
                     func=AF.Exp, bias=negm[:], scale=1.0, inc=[(sem_s, 1)])
                S.wait_cur("vector", sem_s)
                S.op("vector", "tensor_tensor", exm[:], exm[:], sel[:], ALU.mult,
                     inc=[(sem_v, 1)])
                S.wait_cur("vector", sem_v)
                S.op("vector", "reduce_sum", gsum[:], exm[:], axis=AX.X,
                     inc=[(sem_v, 1)])
                S.wait_cur("vector", sem_v)
                S.op("vector", "reciprocal", ginv[:], gsum[:], inc=[(sem_v, 1)])
                S.wait_cur("vector", sem_v)
                S.op("vector", "tensor_single_scalar",
                     comb_tok[:, tt * 15:(tt + 1) * 15], exm[:], ginv[:],
                     ALU.mult, inc=[(sem_v, 1)])
            gating_done = S.mark(sem_v)

            w1_seq = [(u, jp, ab) for u in range(17) for jp in range(8)
                      for ab in range(2)]
            w2_seq = [(u, db) for u in range(17) for db in range(8)]
            w1_dma, w2_dma, w1_free_at, w2_free_at = {}, {}, {}, {}
            w1_next, w2_next = [0], [0]

            def pump_w1(n=1):
                for _ in range(n):
                    seq = w1_next[0]
                    if seq >= len(w1_seq):
                        return
                    u, jp, ab = w1_seq[seq]
                    kind, e = units[u]
                    if seq >= 6:
                        S.wait("gpsimd", sem_t, w1_free_at[seq - 6])
                    w1_dma[seq] = wstream.dma(
                        w1r[:, (seq % 6) * 1024:(seq % 6 + 1) * 1024],
                        w1_src(kind, e, jp, ab))
                    w1_next[0] += 1

            def pump_w2(n=1):
                for _ in range(n):
                    seq = w2_next[0]
                    if seq >= len(w2_seq):
                        return
                    u, db = w2_seq[seq]
                    kind, e = units[u]
                    if seq >= 4:
                        S.wait("gpsimd", sem_t, w2_free_at[seq - 4])
                    w2_dma[seq] = wstream.dma(
                        w2r[:, (seq % 4) * 1024:(seq % 4 + 1) * 1024],
                        w2_src(kind, e, db))
                    w2_next[0] += 1

            pump_w1(6)
            pump_w2(4)
            psY4 = [psY[0], psY[1], psA[0], psA[1]]
            psY4_free = {0: 0, 1: 0, 2: 0, 3: 0}
            crow_map = {}

            def emit_extract_all():
                S.wait("tensor", sem_v, gating_done)
                crow_dma = {}
                for e in range(NE):
                    for g in range(4):
                        S.wait_cur("tensor", sem_s)
                        for q in range(4):
                            tt = g * 4 + q
                            S.op("tensor", "matmul",
                                 aux[0:1, 512 + q * 128:512 + (q + 1) * 128],
                                 comb_tok[:, tt * 15 + e:tt * 15 + e + 1],
                                 ident_sb[:], start=True, stop=True,
                                 inc=[(sem_t, 1)] if q == 3 else None)
                        S.wait_cur("scalar", sem_t)
                        if e > 0:
                            zstream.wait_for("scalar", crow_dma[e - 1])
                        S.op("scalar", "copy",
                             out=combrow[:, g * 512:(g + 1) * 512],
                             in_=aux[0:1, 512:1024], inc=[(sem_s, 1)])
                    S.wait_cur("sync", sem_s)
                    crow_dma[e] = zstream.dma(comb_T[e:e + 1, :], combrow[:])
                crow_map.update(crow_dma)
                return crow_dma[NE - 1]

            def emit_yinit(last_crow):
                zstream.wait_for("tensor", last_crow)
                yfree = {0: psY4_free[0], 1: psY4_free[1]}
                for db in range(8):
                    for c in range(4):
                        par = c % 2
                        S.wait("tensor", sem_v, yfree[par])
                        S.op("tensor", "matmul", psY[par][:],
                             fc2bT_sb[:, db * 128:(db + 1) * 128],
                             comb_T[0:15, c * 512:(c + 1) * 512],
                             start=True, stop=True, inc=[(sem_t, 1)])
                        S.wait_cur("vector", sem_t)
                        S.op("vector", "tensor_single_scalar",
                             ysb[:, db * TH + c * 512:db * TH + (c + 1) * 512],
                             psY[par][:], sfc2b_sb[:, db:db + 1], ALU.add,
                             inc=[(sem_v, 1)])
                        yfree[par] = S.mark(sem_v)
                psY4_free[0] = yfree[0]
                psY4_free[1] = yfree[1]

            def emit_comb(u):
                kind, e = units[u]
                slot = comb_b[u % 2]
                S.wait("sync", sem_t, comb_state["mm"])
                zstream.wait_for("sync", crow_map[e])
                ci = zstream.dma(combrow[:], comb_T[e:e + 1, :])
                zstream.wait_for("tensor", ci)
                S.wait("scalar", sem_v,
                       comb_state["mult"][u % 2])
                for g in range(4):
                    S.wait("tensor", sem_s, comb_state["copy"])
                    S.op("tensor", "matmul", aux[:, 0:512], ones_rbf[:],
                         combrow[:, g * 512:(g + 1) * 512], start=True,
                         stop=True, inc=[(sem_t, 1)])
                    comb_state["mm"] = S.mark(sem_t)
                    S.wait("scalar", sem_t, comb_state["mm"])
                    S.op("scalar", "copy", out=slot[:, g * 512:(g + 1) * 512],
                         in_=aux[:, 0:512], inc=[(sem_s, 1)])
                    comb_state["copy"] = S.mark(sem_s)
                return S.mark(sem_s)

            comb_ready = {}
            psA3 = [psA[0], psA[1], psY[0]]
            psB3 = [psB[0], psB[1], psY[1]]
            psAB_free = {0: (0, 0), 1: (0, 0), 2: (0, 0)}
            gact_free = {0: 0, 1: 0, 2: 0}
            y_prog = {}
            deferred = [0]

            for u, (kind, e) in enumerate(units):
                npar = 2 if u == 0 else 3
                it = 0
                for jp in range(8):
                    seqA = u * 16 + 2 * jp
                    seqB = seqA + 1
                    bA, bB, gmc = bias_cols(kind, e, jp)
                    for c in range(4):
                        par = it % npar
                        if c == 0:
                            wstream.wait_for("tensor", w1_dma[seqA])
                            wstream.wait_for("tensor", w1_dma[seqB])
                        sB, vA = psAB_free[par]
                        S.wait("tensor", sem_s, sB)
                        S.wait("tensor", sem_v, vA)
                        if it < npar:
                            if par < 2:
                                S.wait("tensor", sem_v, psY4_free[2 + par])
                            else:
                                S.wait("tensor", sem_v, psY4_free[0])
                                S.wait("tensor", sem_v, psY4_free[1])
                        it += 1
                        wA = w1r[:, (seqA % 6) * 1024:(seqA % 6 + 1) * 1024]
                        wB = w1r[:, (seqB % 6) * 1024:(seqB % 6 + 1) * 1024]
                        for d in range(8):
                            S.op("tensor", "matmul", psA3[par][:],
                                 wA[:, d * 128:(d + 1) * 128],
                                 xn[:, d * TH + c * 512:d * TH + (c + 1) * 512],
                                 start=(d == 0), stop=(d == 7))
                        for d in range(8):
                            S.op("tensor", "matmul", psB3[par][:],
                                 wB[:, d * 128:(d + 1) * 128],
                                 xn[:, d * TH + c * 512:d * TH + (c + 1) * 512],
                                 start=(d == 0), stop=(d == 7),
                                 inc=[(sem_t, 1)] if d == 7 else None)
                        tmark = S.mark(sem_t)
                        S.wait("scalar", sem_t, tmark)
                        S.wait("scalar", sem_v, gact_free[par])
                        S.op("scalar", "activation", out=g_act[par][:],
                             in_=psB3[par][:], func=AF.Gelu, bias=bB,
                             scale=1.0, inc=[(sem_s, 1)])
                        smark = S.mark(sem_s)
                        S.wait("vector", sem_s, smark)
                        S.op("vector", "tensor_scalar", tmp_a[par][:],
                             psA3[par][:], bA, gmc, ALU.add, ALU.mult)
                        S.op("vector", "tensor_tensor",
                             a_sb[:, jp * TH + c * 512:jp * TH + (c + 1) * 512],
                             tmp_a[par][:], g_act[par][:], ALU.mult,
                             inc=[(sem_v, 1)])
                        psAB_free[par] = (smark, S.mark(sem_v))
                        gact_free[par] = S.mark(sem_v)
                    w1_free_at[seqA] = S.mark(sem_t)
                    w1_free_at[seqB] = S.mark(sem_t)
                    pump_w1(2)
                    if u == 0 and jp == 1 and deferred[0] == 0:
                        deferred[0] = 1
                        last_crow = emit_extract_all()
                    if u == 0 and jp == 4 and deferred[0] == 1:
                        deferred[0] = 2
                        emit_yinit(last_crow)
                        comb_ready[2] = emit_comb(2)
                a_done = S.mark(sem_v)

                if kind == 0:
                    S.wait("vector", sem_s, comb_ready[u])
                    for jp in range(8):
                        S.op("vector", "tensor_tensor",
                             a_sb[:, jp * TH:(jp + 1) * TH],
                             a_sb[:, jp * TH:(jp + 1) * TH],
                             comb_b[u % 2][:], ALU.mult,
                             inc=[(sem_v, 1)] if jp == 7 else None)
                    a_done = S.mark(sem_v)
                    comb_state["mult"][u % 2] = a_done
                if u + 1 < len(units) and units[u + 1][0] == 0:
                    comb_ready[u + 1] = emit_comb(u + 1)

                for db in range(8):
                    seq2 = u * 8 + db
                    wstream.wait_for("tensor", w2_dma[seq2])
                    if db == 0:
                        S.wait("tensor", sem_v, a_done)
                    w2a = w2r[:, (seq2 % 4) * 1024:(seq2 % 4 + 1) * 1024]
                    for c in range(4):
                        par = c
                        bank = psY4[par]
                        S.wait("tensor", sem_v, psY4_free[par])
                        for f in range(8):
                            S.op("tensor", "matmul", bank[:],
                                 w2a[:, f * 128:(f + 1) * 128],
                                 a_sb[:, f * TH + c * 512:f * TH + (c + 1) * 512],
                                 start=(f == 0), stop=(f == 7),
                                 inc=[(sem_t, 1)] if f == 7 else None)
                        S.wait_cur("vector", sem_t)
                        S.op("vector", "tensor_tensor",
                             ysb[:, db * TH + c * 512:db * TH + (c + 1) * 512],
                             ysb[:, db * TH + c * 512:db * TH + (c + 1) * 512],
                             bank[:], ALU.add, inc=[(sem_v, 1)])
                        psY4_free[par] = S.mark(sem_v)
                    w2_free_at[seq2] = S.mark(sem_t)
                    pump_w2(1)
                    y_prog[db] = S.mark(sem_v)

            st_base = n_stores[0]
            res_v = {}
            for d8 in range(8):
                par = d8 % 2
                rio = stg[:, par * 2048:(par + 1) * 2048]
                if d8 >= 2:
                    S.wait("sync", sem_v, res_v[d8 - 2])
                    S.wait("sync", sem_st, 16 * (st_base + d8 - 1))
                else:
                    S.wait("sync", sem_v, xn_ready)
                idx = zstream.dma(rio, z_in[d8 * 128:(d8 + 1) * 128,
                                            half * TH:(half + 1) * TH])
                zstream.wait_for("vector", idx)
                S.wait("vector", sem_v, y_prog[d8])
                S.op("vector", "scalar_tensor_tensor", rio,
                     ysb[:, d8 * TH:(d8 + 1) * TH], 0.5, rio,
                     ALU.mult, ALU.add, inc=[(sem_v, 1)])
                res_v[d8] = S.mark(sem_v)
                S.wait("sync", sem_v, res_v[d8])
                store(z_out[d8 * 128:(d8 + 1) * 128, half * TH:(half + 1) * TH],
                      rio)
        light_barrier()
        ff.close()

    def emit_attn(z_in, z_out, tag="at"):
        slopes = _slopes16()
        ff = contextlib.ExitStack()

        def fsb(name, shape, dtype):
            return ff.enter_context(nc.sbuf_tensor(f"{tag}{name}", list(shape), dtype))

        def fps(name, shape):
            return ff.enter_context(nc.psum_tensor(f"{tag}{name}", list(shape), F32))

        xn = fsb("xn", (128, 8 * T), BF16)
        osb = fsb("o", (128, 8 * T), BF16)
        vbuf = fsb("vb", (128, 32 * 260), BF16)
        qb = fsb("qb", (128, T), BF16)
        kb = fsb("kb", (128, T), BF16)
        pt = [fsb(f"pt{i}", (128, 512), BF16) for i in range(4)]
        invhl = fsb("invhl", (64, T), BF16)
        wqkv = fsb("wqkv", (128, 1024), BF16)
        wo_r = fsb("wo_r", (128, 2 * 1024), BF16)
        wvg = wo_r[:]
        stg = fsb("stg", (128, 4096), F32)
        bvb = fsb("bvb", (128, E), BF16)
        masks_sb = fsb("masks", (128, 4 * 512), BF16)
        mb = fsb("mb", (128, 512), F32)
        rstdb = fsb("rstdb", (128, 512), F32)
        tmpf = fsb("tmpf", (128, 512), F32)
        mrow = rstdb[0:1, 0:512]
        vrow = mb[0:1, 0:512]
        bvrow_l = stg[0:1, 0:E]

        psS = [fps(f"psS{i}", (128, 512)) for i in range(4)]
        psO = fps("psO", (128, 512))
        psQ = fps("psQ", (128, 512))
        psP = [fps(f"psP{i}", (128, 512)) for i in range(2)]

        class _Aux:
            def __getitem__(self, key):
                p, c = key
                if isinstance(c, slice) and c.start == 512:
                    return psS[1][p, c.start - 512:c.stop - 512]
                return psS[0][p, c]
        _aux = _Aux()

        lnB = dict(
            zbf=[osb[:, 0:4096], osb[:, 8192:12288]],
            zsq=[osb[:, 4096:8192], osb[:, 12288:16384]],
            psBc=[(psS[0][:], psS[1][:]), (psS[2][:], psS[3][:])],
            rows=[(rstdb[0:1, 0:512], mb[0:1, 0:512], tmpf[0:1, 0:512]),
                  (stg[0:1, 0:512], stg[0:1, 512:1024],
                   stg[0:1, 1024:1536])],
            lhs=[(ones_r32[0:1, :], ones_r32[0:1, :]),
                 (ones_r32[0:1, :], ones_r32[0:1, :])],
            tmp=[mb[:], rstdb[:]], mv={}, mt={})

        light_barrier()
        wstream.dma(masks_sb[:].rearrange("p (r q) -> p r q", r=4),
                    dt["masks"].rearrange("r p q -> p r q"))
        wstream.dma(bvrow_l[:], dt["bvrow"])
        wstream.wait_all("vector")
        wstream.wait_all("tensor")
        S.wait_cur("tensor", sem_v)
        S.op("tensor", "matmul", psS[0][:], ones_r32[0:1, :],
             bvrow_l[:, 0:512], start=True, stop=True)
        S.op("tensor", "matmul", psS[1][:], ones_r32[0:1, :],
             bvrow_l[:, 512:1024], start=True, stop=True,
             inc=[(sem_t, 1)])
        S.wait_cur("scalar", sem_t)
        S.op("scalar", "copy", out=bvb[:, 0:512], in_=psS[0][:])
        S.op("scalar", "copy", out=bvb[:, 512:1024], in_=psS[1][:],
             inc=[(sem_s, 1)])
        S.wait_cur("tensor", sem_s)

        psVp = [psP[1], psQ]
        psVp_hold = {0: {"v": 0, "s": 0}, 1: {"v": 0, "s": 0}}
        psO_free = {0: 0, 1: 0}
        pt_free = {0: 0, 1: 0, 2: 0, 3: 0}
        psS_war = {0: 0, 1: 0}
        pend_norm = [None]
        pend_epi = [None]
        last_S_mark = 0

        def flush_epi():
            if pend_epi[0] is None:
                return
            e_ps, e_ob, e_dt, e_pb, e_qc, e_pv = pend_epi[0]
            pend_epi[0] = None
            S.wait("vector", sem_t, e_pv)
            S.op("vector", "tensor_copy",
                 invhl[0:1, e_qc * 512:(e_qc + 1) * 512], e_ps[64:65, :])
            S.op("vector", "tensor_tensor",
                 invhl[32:33, e_qc * 512:(e_qc + 1) * 512],
                 e_ps[64:65, :],
                 invhl[0:1, e_qc * 512:(e_qc + 1) * 512],
                 ALU.subtract)
            S.op("vector", "tensor_copy",
                 osb[e_pb:e_pb + 64,
                     e_dt * T + e_qc * 512:e_dt * T + (e_qc + 1) * 512],
                 e_ps[0:64, :], inc=[(sem_v, 1)])
            psO_free[e_ob] = S.mark(sem_v)

        def emit_norm():
            if pend_norm[0] is None:
                return
            ndt, npb = pend_norm[0]
            pend_norm[0] = None
            S.wait_cur("tensor", sem_v)
            for nqc in range(8):
                par = nqc % 2
                bank = psS[2 + par]
                S.wait("tensor", sem_s, pt_free[2 + par])
                S.wait("tensor", sem_v, psS_war[par])
                S.op("tensor", "matmul", bank[0:64, :],
                     selq_sb[0:1, 0:64],
                     invhl[0:1, nqc * 512:(nqc + 1) * 512],
                     start=True, stop=True)
                S.op("tensor", "matmul", bank[64:128, :],
                     selq_sb[32:33, 0:64],
                     invhl[32:33, nqc * 512:(nqc + 1) * 512],
                     start=True, stop=True, inc=[(sem_t, 1)])
                nm = S.mark(sem_t)
                S.wait("vector", sem_t, nm)
                S.op("vector", "tensor_copy", tmpf[npb:npb + 64, :],
                     bank[64:128, :])
                S.op("vector", "tensor_tensor", tmpf[npb:npb + 64, :],
                     tmpf[npb:npb + 64, :], bank[0:64, :], ALU.add)
                S.op("vector", "reciprocal", tmpf[npb:npb + 64, :],
                     tmpf[npb:npb + 64, :])
                S.op("vector", "tensor_tensor",
                     osb[npb:npb + 64,
                         ndt * T + nqc * 512:ndt * T + (nqc + 1) * 512],
                     osb[npb:npb + 64,
                         ndt * T + nqc * 512:ndt * T + (nqc + 1) * 512],
                     tmpf[npb:npb + 64, :], ALU.mult,
                     inc=[(sem_v, 1)])
                psS_war[par] = S.mark(sem_v)
        S.op("vector", "memset", qb[64:70, :], 1.0)
        S.op("vector", "memset", kb[64:70, :], 1.0)
        S.op("vector", "memset", vbuf[:], 1.0, inc=[(sem_v, 1)])
        init_done = S.mark(sem_v)
        S.wait("gpsimd", sem_v, init_done)
        vw0_idx = wstream.dma(wvg[:], dt["wvs"][0].rearrange("p d j -> p (d j)"))
        wstream.wait_for("tensor", vw0_idx)
        for tc in range(8):
            def xn_dst(d, tc=tc):
                return xn[:, d * T + tc * 512:d * T + (tc + 1) * 512]
            emit_ln_chunk(z_in, tc, xn_dst, lnB)
            for tt in range(4 * tc, 4 * tc + 4):
                par = tt % 2
                pb = psVp[par]
                S.wait("tensor", sem_v, lnB["mv"][tc])
                S.wait("tensor", sem_v, psVp_hold[par]["v"])
                S.wait("tensor", sem_s, psVp_hold[par]["s"])
                for d in range(8):
                    S.op("tensor", "matmul", pb[:, 0:256],
                         xn[:, d * T + tt * 128:d * T + (tt + 1) * 128],
                         wvg[:, d * 256:(d + 1) * 256],
                         start=(d == 0), stop=(d == 7),
                         inc=[(sem_t, 1)] if d == 7 else None)
                S.wait_cur("vector", sem_t)
                dst = vbuf[:, tt * 260:(tt + 1) * 260].rearrange(
                    "p (hh c) -> p hh c", hh=4)[:, :, 0:64]
                S.op("vector", "tensor_tensor", dst,
                     pb[:, 0:256].rearrange("p (hh c) -> p hh c", hh=4),
                     bvb[:, 0:256].rearrange("p (hh c) -> p hh c", hh=4),
                     ALU.add, inc=[(sem_v, 1)])
                psVp_hold[par]["v"] = S.mark(sem_v)
        xn_ready = S.mark(sem_v)

        def kt_min(h, qc):
            d = int(WIN_C / slopes[h])
            return max(0, (qc * 512 - d - 127) // 128)

        for g in range(4):
            if g > 0:
                vw_idx = wstream.dma(wvg[:],
                                     dt["wvs"][g].rearrange("p d j -> p (d j)"))
                wstream.wait_for("tensor", vw_idx)
                for tt in range(32):
                    par = tt % 2
                    pb = psVp[par]
                    S.wait("tensor", sem_v, psVp_hold[par]["v"])
                    S.wait("tensor", sem_s, psVp_hold[par]["s"])
                    for d in range(8):
                        S.op("tensor", "matmul", pb[:, 0:256],
                             xn[:, d * T + tt * 128:d * T + (tt + 1) * 128],
                             wvg[:, d * 256:(d + 1) * 256],
                             start=(d == 0), stop=(d == 7),
                             inc=[(sem_t, 1)] if d == 7 else None)
                    S.wait_cur("vector", sem_t)
                    dst = vbuf[:, tt * 260:(tt + 1) * 260].rearrange(
                        "p (hh c) -> p hh c", hh=4)[:, :, 0:64]
                    S.op("vector", "tensor_tensor", dst,
                         pb[:, 0:256].rearrange("p (hh c) -> p hh c", hh=4),
                         bvb[:, g * 256:(g + 1) * 256].rearrange(
                             "p (hh c) -> p hh c", hh=4),
                         ALU.add, inc=[(sem_v, 1)])
                    psVp_hold[par]["v"] = S.mark(sem_v)

            for h in range(4 * g, 4 * g + 4):
                dt_h = h // 2
                pbase = (h % 2) * 64
                S.wait("gpsimd", sem_t, last_S_mark)
                qk_idx = wstream.dma(wqkv[:],
                                     dt["wqks"][h].rearrange("p d j -> p (d j)"))
                aq_idx = wstream.dma(qb[64:67, :], dt["alq"][h])
                ak_idx = wstream.dma(kb[67:70, :], dt["alk"][h])
                wstream.wait_for("tensor", qk_idx)
                for cc in range(8):
                    par = cc % 2
                    pb = psVp[par]
                    S.wait("tensor", sem_s, psVp_hold[par]["s"])
                    S.wait("tensor", sem_v, psVp_hold[par]["v"])
                    for d in range(8):
                        S.op("tensor", "matmul", pb[:],
                             wqkv[:, d * 128:(d + 1) * 128],
                             xn[:, d * T + cc * 512:d * T + (cc + 1) * 512],
                             start=(d == 0), stop=(d == 7),
                             inc=[(sem_t, 1)] if d == 7 else None)
                    pj_mm = S.mark(sem_t)
                    S.wait("scalar", sem_t, pj_mm)
                    S.op("scalar", "activation",
                         out=qb[0:64, cc * 512:(cc + 1) * 512],
                         in_=pb[0:64, :], func=AF.Identity,
                         bias=bqk_sb[0:64, h:h + 1], scale=1.0)
                    S.op("scalar", "activation",
                         out=kb[0:64, cc * 512:(cc + 1) * 512],
                         in_=pb[64:128, :], func=AF.Identity,
                         bias=bqk_sb[64:128, h:h + 1],
                         scale=1.0, inc=[(sem_s, 1)])
                    psVp_hold[par]["s"] = S.mark(sem_s)
                proj_done = S.mark(sem_s)
                wstream.wait_for("tensor", aq_idx)
                wstream.wait_for("tensor", ak_idx)
                emit_norm()
                S.wait("tensor", sem_s, proj_done)

                for qc in range(8):
                    kths = list(range(kt_min(h, qc), 4 * qc + 4))
                    nk = len(kths)
                    ob = qc % 2
                    psOb = psO if ob == 0 else psP[0]
                    emarks = {}
                    LA = 2
                    for idx in range(nk + LA):
                        if idx == 2:
                            flush_epi()
                        if idx < nk:
                            kt = kths[idx]
                            par = idx % 4
                            S.wait("tensor", sem_s, pt_free[par])
                            if par >= 2:
                                S.wait("tensor", sem_v, psS_war[par - 2])
                            S.op("tensor", "matmul", psS[par][:],
                                 kb[0:70, kt * 128:(kt + 1) * 128],
                                 qb[0:70, qc * 512:(qc + 1) * 512],
                                 start=True, stop=True, inc=[(sem_t, 1)])
                            s_mm = S.mark(sem_t)
                            last_S_mark = s_mm
                            r = kt * 128 - qc * 512
                            if r >= 0:
                                ri = r // 128
                                S.wait("vector", sem_t, s_mm)
                                S.op("vector", "tensor_tensor", psS[par][:],
                                     psS[par][:],
                                     masks_sb[:, ri * 512:(ri + 1) * 512],
                                     ALU.add, inc=[(sem_v, 1)])
                                S.wait_cur("scalar", sem_v)
                            else:
                                S.wait("scalar", sem_t, s_mm)
                            S.op("scalar", "activation", out=pt[par][:],
                                 in_=psS[par][:], func=AF.Exp,
                                 inc=[(sem_s, 1)])
                            emarks[idx] = S.mark(sem_s)
                        j = idx - LA
                        if j >= 0:
                            kt_j = kths[j]
                            par_j = j % 4
                            S.wait("tensor", sem_s, emarks[j])
                            if j == 0:
                                S.wait("tensor", sem_v, psO_free[ob])
                            S.op("tensor", "matmul", psOb[0:65, :],
                                 vbuf[:, kt_j * 260 + (h % 4) * 65:
                                      kt_j * 260 + (h % 4) * 65 + 65],
                                 pt[par_j][:], start=(j == 0),
                                 stop=(j == nk - 1),
                                 inc=[(sem_t, 1)] if j == nk - 1 else None)
                            pt_free[par_j] = emarks[j]
                    pend_epi[0] = (psOb, ob, dt_h, pbase, qc,
                                   S.mark(sem_t))
                flush_epi()
                pend_norm[0] = (dt_h, pbase)
        emit_norm()
        o_done = S.mark(sem_v)

        S.wait("tensor", sem_v, o_done)
        S.wait_cur("gpsimd", sem_t)
        wo_dma = {}
        wo_seq = 0
        wo_free = {}

        def pump_wo(jt_seq):
            nonlocal wo_seq
            if jt_seq >= 2:
                S.wait("gpsimd", sem_t, wo_free[jt_seq - 2])
            wo_dma[jt_seq] = wstream.dma(
                wo_r[:, (jt_seq % 2) * 1024:(jt_seq % 2 + 1) * 1024],
                dt["wos"][jt_seq % 8].rearrange("p d j -> p (d j)"))

        pump_wo(0)
        pump_wo(1)
        psW_free = {0: 0, 1: 0}
        ring = [mb[:], rstdb[:], tmpf[:]]
        ring_v = {}
        ld_idx = {}
        st_of = {}

        def emit_rload(k):
            tcx, jtx = divmod(k, 8)
            if k >= 3:
                S.wait("sync", sem_v, ring_v[k - 3])
            else:
                S.wait("sync", sem_v, o_done)
            ld_idx[k] = zstream.dma(
                ring[k % 3], z_in[jtx * 128:(jtx + 1) * 128,
                                  tcx * 512:(tcx + 1) * 512])

        for k in range(3):
            emit_rload(k)
        for tc in range(8):
            for jt in range(8):
                jt_seq = tc * 8 + jt
                wstream.wait_for("tensor", wo_dma[jt_seq])
                par = jt % 2
                woa = wo_r[:, (jt_seq % 2) * 1024:(jt_seq % 2 + 1) * 1024]
                S.wait("tensor", sem_v, psW_free[par])
                for d in range(8):
                    S.op("tensor", "matmul", psS[par][:],
                         woa[:, d * 128:(d + 1) * 128],
                         osb[:, d * T + tc * 512:d * T + (tc + 1) * 512],
                         start=(d == 0), stop=(d == 7),
                         inc=[(sem_t, 1)] if d == 7 else None)
                wo_free[jt_seq] = S.mark(sem_t)
                if jt_seq + 2 < 64:
                    pump_wo(jt_seq + 2)
                zstream.wait_for("vector", ld_idx[jt_seq])
                if tc > 0:
                    S.wait("vector", sem_st,
                           16 * (st_of[jt_seq - 8] + 1))
                S.wait("vector", sem_t, wo_free[jt_seq])
                S.op("vector", "scalar_tensor_tensor",
                     stg[:, jt * 512:(jt + 1) * 512], psS[par][:],
                     bo_sb[:, jt:jt + 1], ring[jt_seq % 3],
                     ALU.add, ALU.add, inc=[(sem_v, 1)])
                ring_v[jt_seq] = S.mark(sem_v)
                psW_free[par] = ring_v[jt_seq]
                S.wait("sync", sem_v, ring_v[jt_seq])
                st_of[jt_seq] = n_stores[0]
                store(z_out[jt * 128:(jt + 1) * 128,
                            tc * 512:(tc + 1) * 512],
                      stg[:, jt * 512:(jt + 1) * 512])
                if jt_seq + 3 < 64:
                    emit_rload(jt_seq + 3)
        light_barrier()
        ff.close()

    block = es.enter_context(nc.Block())
    for p in parts:
        z_in, z_out = io[p]
        if p in ("ffn1", "ffn2"):
            emit_ffn(z_in, z_out, "f1" if p == "ffn1" else "f2")
        else:
            emit_attn(z_in, z_out)
    S.wait("sync", sem_st, 16 * n_stores[0])
    S.replay(block)
    es.close()
    return nc


def _host_shared(i):
    f32 = np.float32
    out = {}
    fc1_w = np.asarray(i["L0_ffn_fc1_w"], f32)
    out["w1s"] = np.ascontiguousarray(
        fc1_w.reshape(NE, 16, 128, 8, 128).transpose(0, 1, 4, 3, 2))
    fc2_w = np.asarray(i["L0_ffn_fc2_w"], f32)
    out["w2s"] = np.ascontiguousarray(
        fc2_w.reshape(NE, 8, 128, 8, 128).transpose(0, 1, 4, 3, 2))
    s1 = np.asarray(i["L0_ffn_s_fc1_w"], f32)
    out["ws1s"] = np.ascontiguousarray(
        s1.reshape(32, 128, 8, 128).transpose(0, 3, 2, 1))
    s2 = np.asarray(i["L0_ffn_s_fc2_w"], f32)
    out["ws2s"] = np.ascontiguousarray(
        s2.reshape(8, 128, 16, 128).transpose(0, 3, 2, 1))
    gw = np.asarray(i["L0_ffn_gate_w"], f32)
    out["gws"] = np.ascontiguousarray(gw.reshape(15, 8, 128).transpose(2, 1, 0))
    wq8 = np.asarray(i["L0_attn_Wq"], f32) * 0.125
    wk = np.asarray(i["L0_attn_Wk"], f32)
    qpart = wq8.reshape(16, 64, 8, 128).transpose(0, 3, 2, 1)
    kpart = wk.reshape(16, 64, 8, 128).transpose(0, 3, 2, 1)
    out["wqks"] = np.ascontiguousarray(
        np.concatenate([qpart, kpart], axis=3))
    wv = np.asarray(i["L0_attn_Wv"], f32)
    out["wvs"] = np.ascontiguousarray(
        wv.reshape(4, 256, 8, 128).transpose(0, 3, 2, 1))
    wo = np.asarray(i["L0_attn_Wo"], f32)
    out["wos"] = np.ascontiguousarray(
        wo.reshape(8, 128, 8, 128).transpose(0, 3, 2, 1))
    out["fc1b"] = np.ascontiguousarray(
        np.asarray(i["L0_ffn_fc1_b"], f32).reshape(NE, 16, 128)
        .transpose(2, 0, 1).reshape(128, NE * 16))
    out["fc2bT"] = np.ascontiguousarray(np.asarray(i["L0_ffn_fc2_b"], f32))
    out["gmult"] = np.ascontiguousarray(
        np.asarray(i["L0_ffn_geglu_mult"], f32).reshape(NE, 8, 128)
        .transpose(2, 0, 1).reshape(128, NE * 8))
    out["sfc1b"] = np.ascontiguousarray(
        np.asarray(i["L0_ffn_s_fc1_b"], f32).reshape(32, 128).T)
    out["sgm"] = np.ascontiguousarray(
        np.asarray(i["L0_ffn_s_geglu_mult"], f32).reshape(16, 128).T)
    out["sfc2b"] = np.ascontiguousarray(
        np.asarray(i["L0_ffn_s_fc2_b"], f32).reshape(8, 128).T)
    bq8 = np.asarray(i["L0_attn_bq"], f32).reshape(16, 64) * 0.125
    bk_ = np.asarray(i["L0_attn_bk"], f32).reshape(16, 64)
    out["bqkv"] = np.ascontiguousarray(
        np.concatenate([bq8, bk_], axis=1).T)
    selq = np.zeros((128, 128), f32)
    selq[0, 0:64] = 1.0
    selq[32, 0:64] = 1.0
    out["selqv"] = selq
    out["bvrow"] = np.asarray(i["L0_attn_bv"], f32).reshape(1, E)
    out["bov"] = np.ascontiguousarray(np.asarray(i["L0_attn_bo"], f32).reshape(8, 128).T)

    slopes = _slopes16().astype(np.float64)
    pos = np.arange(T, dtype=np.float64)
    alq = np.zeros((H, 3, T), np.float32)
    alk = np.zeros((H, 3, T), np.float32)
    for h in range(H):
        alq[h] = _tri_split(-slopes[h] * pos)
        alk[h] = _tri_split(slopes[h] * pos)
    out["alq"] = alq
    out["alk"] = alk
    masks = np.zeros((4, 128, 512), np.float32)
    kk = np.arange(128)[:, None]
    qq = np.arange(512)[None, :]
    for r in range(4):
        masks[r] = np.where(kk + r * 128 <= qq, 0.0, NEG)
    out["masks"] = masks
    out["identb"] = np.eye(128, dtype=np.float32)
    sent = np.array([-(1e30 * (1.0 + 0.01 * k)) for k in range(1, 9)], np.float32)
    out["sentv"] = np.tile(sent, (128, 1))
    return out


_CACHED = {}


def kernel(**inputs):
    parts = _CACHED.get("parts", ("ffn1", "attn", "ffn2"))
    key = ("nc", parts)
    if key not in _CACHED:
        _CACHED[key] = build_program(parts)
    nc = _CACHED[key]
    shared = _host_shared(inputs)
    x = np.asarray(inputs["L0_attn_x"], np.float32)
    in_maps = []
    for b in range(8):
        m = dict(shared)
        m["xT"] = np.ascontiguousarray(x[b].T)
        in_maps.append(m)
    res = run_bass_kernel_spmd(nc, in_maps, core_ids=list(range(8)))
    out = np.stack([np.asarray(r["outT"]).T for r in res.results])
    return np.ascontiguousarray(out.astype(np.float32))


# version 22 (final)
# speedup vs baseline: 1.1909x; 1.1062x over previous
import contextlib
import math

import numpy as np
import ml_dtypes
import concourse.bass as bass
import concourse.mybir as mybir
from concourse.bass_utils import run_bass_kernel_spmd

F32 = mybir.dt.float32
BF16 = mybir.dt.bfloat16
AF = mybir.ActivationFunctionType
ALU = mybir.AluOpType
AX = mybir.AxisListType

E = 1024
T = 4096
H = 16
DH = 64
NE = 15
TH = 2048
NEG = -1.0e30
WIN_C = 45.0

ENGS = ("tensor", "vector", "scalar", "sync", "gpsimd")


def _slopes16():
    start = 2.0 ** (-2.0 ** (-(math.log2(16) - 3)))
    return np.array([start * start ** i for i in range(16)], dtype=np.float32)


def _tri_split(v):
    hi = v.astype(ml_dtypes.bfloat16).astype(np.float64)
    r = v - hi
    mid = r.astype(ml_dtypes.bfloat16).astype(np.float64)
    lo = (r - mid).astype(ml_dtypes.bfloat16).astype(np.float64)
    return np.stack([hi, mid, lo]).astype(np.float32)


class Sched:
    def __init__(self, nc):
        self.nc = nc
        self.ops = {e: [] for e in ENGS}
        self.cnt = {}

    def sem(self, handle):
        self.cnt[handle.name] = 0
        return handle

    def op(self, eng, method, *args, inc=None, **kw):
        incs = []
        if inc:
            for s, n in inc:
                self.cnt[s.name] += n
                incs.append((s, n))
        self.ops[eng].append(("op", method, args, kw, incs))

    def mark(self, sem):
        return self.cnt[sem.name]

    def wait(self, eng, sem, val):
        if val > 0:
            self.ops[eng].append(("wait", sem, val))

    def wait_cur(self, eng, sem):
        self.wait(eng, sem, self.cnt[sem.name])

    def replay(self, block):
        for eng in ENGS:
            lst = self.ops[eng]

            def body(handle, lst=lst):
                for d in lst:
                    if d[0] == "op":
                        _, method, args, kw, incs = d
                        ins = getattr(handle, method)(*args, **kw)
                        for s, n in incs:
                            ins.then_inc(s, n)
                    else:
                        handle.wait_ge(d[1], d[2])
            getattr(block, eng)(body)


class DmaStream:
    def __init__(self, sched, eng, sems):
        self.s, self.eng, self.sems, self.n = sched, eng, sems, 0

    def dma(self, out, in_, **kw):
        sem = self.sems[self.n % len(self.sems)]
        self.s.op(self.eng, "dma_start", inc=[(sem, 16)], out=out, in_=in_, **kw)
        self.n += 1
        return self.n - 1

    def wait_for(self, eng, idx):
        k = len(self.sems)
        self.s.wait(eng, self.sems[idx % k], 16 * (idx // k + 1))

    def wait_all(self, eng):
        for i, sem in enumerate(self.sems):
            tot = (self.n - 1 - i) // len(self.sems) + 1 if self.n > i else 0
            if tot:
                self.s.wait(eng, sem, 16 * tot)


def build_program(parts=("ffn1", "attn", "ffn2")):
    nc = bass.Bass("TRN2", target_bir_lowering=False, debug=False)
    dt = {}

    def din(name, shape):
        dt[name] = nc.dram_tensor(name, list(shape), F32, kind="ExternalInput").ap()

    din("xT", (E, T))
    din("w1s", (NE, 16, 128, 8, 128))
    din("w2s", (NE, 8, 128, 8, 128))
    din("ws1s", (32, 128, 8, 128))
    din("ws2s", (8, 128, 16, 128))
    din("gws", (128, 8, 15))
    din("wqks", (H, 128, 8, 128))
    din("wvs", (4, 128, 8, 4 * DH))
    din("wos", (8, 128, 8, 128))
    din("fc1b", (128, NE * 16))
    din("fc2bT", (NE, E))
    din("gmult", (128, NE * 8))
    din("sfc1b", (128, 32))
    din("sgm", (128, 16))
    din("sfc2b", (128, 8))
    din("bqkv", (128, H))
    din("selqv", (128, 128))
    din("bvrow", (1, E))
    din("bov", (128, 8))
    din("alq", (H, 3, T))
    din("alk", (H, 3, T))
    din("masks", (4, 128, 512))
    din("identb", (128, 128))
    din("sentv", (128, 8))
    outT = nc.dram_tensor("outT", [E, T], F32, kind="ExternalOutput").ap()
    z1T = nc.dram_tensor("z1T", [E, T], F32, kind="Internal").ap()
    z2T = nc.dram_tensor("z2T", [E, T], F32, kind="Internal").ap()

    chain = [dt["xT"], z1T, z2T]
    io = {}
    for i, p in enumerate(parts):
        io[p] = (chain[i], outT if i == len(parts) - 1 else chain[i + 1])

    es = contextlib.ExitStack()

    def sb(name, shape, dtype):
        return es.enter_context(nc.sbuf_tensor(name, list(shape), dtype))

    bqk_sb = sb("bqk_sb", (128, H), F32)
    selq_sb = sb("selq_sb", (128, 128), BF16)
    bo_sb = sb("bo_sb", (128, 8), F32)
    ones_r32 = sb("ones_r32", (65, 128), F32)
    ones_rbf = sb("ones_rbf", (1, 128), BF16)
    ones_cbf = sb("ones_cbf", (128, 1), BF16)
    eps_sb = sb("eps_sb", (1, 1), F32)

    S = Sched(nc)

    def mksem(name):
        return S.sem(es.enter_context(nc.semaphore(name)))

    sem_t = mksem("sem_t")
    sem_v = mksem("sem_v")
    sem_s = mksem("sem_s")
    sem_w = [mksem(f"sem_w{i}") for i in range(4)]
    sem_zl = [mksem(f"sem_zl{i}") for i in range(2)]
    sem_st = mksem("sem_st")
    wstream = DmaStream(S, "gpsimd", sem_w)
    zstream = DmaStream(S, "sync", sem_zl)
    n_stores = [0]

    def store(out, in_):
        S.op("sync", "dma_start", inc=[(sem_st, 16)], out=out, in_=in_)
        n_stores[0] += 1

    def light_barrier():
        for e in ENGS:
            for sem in (sem_t, sem_v, sem_s):
                S.wait_cur(e, sem)
            wstream.wait_all(e)
            zstream.wait_all(e)
            S.wait(e, sem_st, 16 * n_stores[0])

    for dst, src in [(bqk_sb, "bqkv"), (selq_sb, "selqv"),
                     (bo_sb, "bov")]:
        wstream.dma(dst[:], dt[src])
    wstream.wait_all("vector")
    S.op("vector", "memset", ones_r32[:], 1.0)
    S.op("vector", "memset", ones_rbf[:], 1.0)
    S.op("vector", "memset", ones_cbf[:], 1.0)
    S.op("vector", "memset", eps_sb[:], 1e-5, inc=[(sem_v, 1)])
    for e in ("tensor", "scalar", "sync", "gpsimd"):
        wstream.wait_all(e)
        S.wait_cur(e, sem_v)

    def emit_ln_chunk(z_in, tc, xn_dst, B):
        p = tc % 2
        zbf, zsq = B["zbf"][p], B["zsq"][p]
        bc0, bc1 = B["psBc"][p]
        st1, st2 = bc0[0:1, 0:512], bc1[0:1, 0:512]
        mrow, vrow, m2row = B["rows"][p]
        tmp = B["tmp"][p]
        mv, mt = B["mv"], B["mt"]
        src = z_in.rearrange("(dt p) t -> p dt t", p=128)[:, :,
                                                          tc * 512:(tc + 1) * 512]
        stgL = B.get("stgL")
        if stgL is not None:
            sl = stgL[p]
            if tc >= 2:
                S.wait("sync", sem_v, mv[tc - 2])
            S.wait("sync", sem_st, 16 * n_stores[0])
            idx = zstream.dma(sl.rearrange("p (dt t) -> p dt t", dt=8), src)
            zstream.wait_for("scalar", idx)
            if tc >= 2:
                S.wait("scalar", sem_t, mt[tc - 2])
            S.op("scalar", "copy", out=zbf[:], in_=sl[:])
            S.op("scalar", "activation", out=zsq[:], in_=sl[:],
                 func=AF.Square, inc=[(sem_s, 1)])
            sub_src = sl
        else:
            if tc >= 2:
                S.wait("gpsimd", sem_v, mv[tc - 2])
            idx = wstream.dma(zbf.rearrange("p (dt t) -> p dt t", dt=8), src)
            wstream.wait_for("scalar", idx)
            if tc >= 2:
                S.wait("scalar", sem_t, mt[tc - 2])
            S.op("scalar", "activation", out=zsq[:], in_=zbf[:],
                 func=AF.Square, inc=[(sem_s, 1)])
            sub_src = zbf
        sq_m = S.mark(sem_s)
        S.wait("tensor", sem_s, sq_m)
        if tc >= 2:
            S.wait("tensor", sem_v, mv[tc - 2])
        for d in range(8):
            S.op("tensor", "matmul", st1, ones_cbf[:],
                 zbf[:, d * 512:(d + 1) * 512], start=(d == 0), stop=(d == 7))
        for d in range(8):
            S.op("tensor", "matmul", st2, ones_cbf[:],
                 zsq[:, d * 512:(d + 1) * 512], start=(d == 0), stop=(d == 7),
                 inc=[(sem_t, 1)] if d == 7 else None)
        mt[tc] = S.mark(sem_t)
        S.wait("vector", sem_t, mt[tc])
        S.op("vector", "tensor_scalar_mul", mrow, st1, 1.0 / 1024.0)
        S.op("vector", "tensor_tensor", m2row, mrow, mrow, ALU.mult)
        S.op("vector", "scalar_tensor_tensor", vrow, st2, 1.0 / 1024.0,
             m2row, ALU.mult, ALU.subtract, inc=[(sem_v, 1)])
        row_m = S.mark(sem_v)
        S.wait("scalar", sem_v, row_m)
        S.op("scalar", "activation", out=vrow, in_=vrow,
             func=AF.Ln, bias=eps_sb[:], scale=1.0)
        S.op("scalar", "activation", out=vrow, in_=vrow,
             func=AF.Exp, scale=-0.5, inc=[(sem_s, 1)])
        ex_m = S.mark(sem_s)
        lm, lv = B["lhs"][p]
        S.wait("tensor", sem_s, ex_m)
        S.op("tensor", "matmul", bc0[:, 0:512], lm, mrow,
             start=True, stop=True)
        S.op("tensor", "matmul", bc1[:, 0:512], lv, vrow,
             start=True, stop=True, inc=[(sem_t, 1)])
        bc_m = S.mark(sem_t)
        S.wait("vector", sem_t, bc_m)
        for d in range(8):
            S.op("vector", "tensor_tensor", tmp[:],
                 sub_src[:, d * 512:(d + 1) * 512], bc0[:, 0:512],
                 ALU.subtract)
            S.op("vector", "tensor_tensor", xn_dst(d), tmp[:],
                 bc1[:, 0:512], ALU.mult,
                 inc=[(sem_v, 1)] if d == 7 else None)
        mv[tc] = S.mark(sem_v)

    def emit_ffn(z_in, z_out, tag):
        ff = contextlib.ExitStack()

        def fsb(name, shape, dtype):
            return ff.enter_context(nc.sbuf_tensor(f"{tag}{name}", list(shape), dtype))

        def fps(name, shape):
            return ff.enter_context(nc.psum_tensor(f"{tag}{name}", list(shape), F32))

        xn = fsb("xn", (128, 8 * TH), BF16)
        ysb = fsb("y", (128, 8 * TH), F32)
        a_sb = fsb("a", (128, 8 * TH), BF16)
        stg = fsb("stg", (128, 4096), F32)
        fc1b_sb = fsb("fc1b_sb", (128, NE * 16), F32)
        gm_sb = fsb("gm_sb", (128, NE * 8), F32)
        sfc1b_sb = fsb("sfc1b_sb", (128, 32), F32)
        sgm_sb = fsb("sgm_sb", (128, 16), F32)
        sfc2b_sb = fsb("sfc2b_sb", (128, 8), F32)
        gate_sb = fsb("gate_sb", (128, 8 * 15), BF16)
        ident_sb = fsb("ident_sb", (128, 128), BF16)
        sent_sb = fsb("sent_sb", (128, 8), F32)
        w1r = fsb("w1r", (128, 6 * 1024), BF16)
        w2r = fsb("w2r", (128, 4 * 1024), BF16)
        comb_b = [fsb(f"combb{i}", (128, TH), BF16) for i in range(2)]
        combrow = fsb("combrow", (1, TH), BF16)
        comb_T = fsb("comb_T", (16, TH), BF16)
        fc2bT_sb = fsb("fc2bT_sb", (NE, E), BF16)
        comb_tok = fsb("comb_tok", (128, 16 * 15), BF16)
        mb = fsb("mb", (128, 512), F32)
        rstdb = fsb("rstdb", (128, 512), F32)
        tmpf = fsb("tmpf", (128, 512), F32)
        g_act = [fsb(f"gact{i}", (128, 512), BF16) for i in range(3)]
        tmp_a = [fsb(f"tmpa{i}", (128, 512), BF16) for i in range(3)]
        lgt_all = fsb("lgt_all", (128, 240), F32)
        work = fsb("work", (128, 23), F32)
        wout = fsb("wout", (128, 23), F32)
        m8 = fsb("m8", (128, 8), F32)
        rep = fsb("rep", (128, 8), F32)
        sel = fsb("sel", (128, 15), F32)
        exm = fsb("exm", (128, 15), F32)
        negm = fsb("negm", (128, 1), F32)
        gsum = fsb("gsum", (128, 1), F32)
        ginv = fsb("ginv", (128, 1), F32)

        psA = [fps(f"psA{i}", (128, 512)) for i in range(2)]
        psB = [fps(f"psB{i}", (128, 512)) for i in range(2)]
        psY = [fps(f"psY{i}", (128, 512)) for i in range(2)]
        aux = fps("aux", (128, 1024))

        lnB = dict(
            stgL=[stg[:, 0:4096], ysb[:, 0:4096]],
            zbf=[a_sb[:, 0:4096], a_sb[:, 8192:12288]],
            zsq=[a_sb[:, 4096:8192], a_sb[:, 12288:16384]],
            psBc=[(aux[:, 0:512], aux[:, 512:1024]),
                  (psA[0][:], psA[1][:])],
            rows=[(rstdb[0:1, 0:512], mb[0:1, 0:512], tmpf[0:1, 0:512]),
                  (tmpf[64:65, 0:512], tmpf[32:33, 0:512],
                   tmpf[96:97, 0:512])],
            lhs=[(ones_r32[0:1, :], ones_r32[0:1, :]),
                 (ones_r32[64:65, :], ones_r32[32:33, :])],
            tmp=[mb[:], rstdb[:]], mv={}, mt={})

        units = [(1, 0), (2, 0)] + [(0, e) for e in range(NE)]

        def w1_src(kind, e, jp, ab):
            if kind == 0:
                return dt["w1s"][e, jp + 8 * ab].rearrange("p d f -> p (d f)")
            j = (jp + 8 * (kind - 1)) + 16 * ab
            return dt["ws1s"][j].rearrange("p d f -> p (d f)")

        def w2_src(kind, e, db):
            if kind == 0:
                return dt["w2s"][e, db].rearrange("p f d -> p (f d)")
            f0 = 8 * (kind - 1)
            return dt["ws2s"][db, :, f0:f0 + 8, :].rearrange("p f d -> p (f d)")

        def bias_cols(kind, e, jp):
            if kind == 0:
                return (fc1b_sb[:, e * 16 + jp:e * 16 + jp + 1],
                        fc1b_sb[:, e * 16 + 8 + jp:e * 16 + 9 + jp],
                        gm_sb[:, e * 8 + jp:e * 8 + jp + 1])
            base = 8 * (kind - 1)
            return (sfc1b_sb[:, base + jp:base + jp + 1],
                    sfc1b_sb[:, 16 + base + jp:17 + base + jp],
                    sgm_sb[:, base + jp:base + jp + 1])

        for dst, src in [(fc1b_sb, "fc1b"), (gm_sb, "gmult"),
                         (sfc1b_sb, "sfc1b"), (sgm_sb, "sgm"),
                         (sfc2b_sb, "sfc2b"), (gate_sb, "gws"),
                         (ident_sb, "identb"), (sent_sb, "sentv")]:
            ap = dt[src]
            if src == "gws":
                ap = ap.rearrange("p d e -> p (d e)")
            wstream.dma(dst[:], ap)
        for e in ("tensor", "vector", "scalar"):
            wstream.wait_all(e)
        fc2bT_dma = wstream.dma(fc2bT_sb[:], dt["fc2bT"])
        wstream.wait_for("tensor", fc2bT_dma)
        for half in range(2):
            light_barrier()
            comb_state = {"mm": 0, "copy": 0, "mult": {0: 0, 1: 0}}

            for c in range(4):
                tc = half * 4 + c

                def xn_dst(d, c=c):
                    return xn[:, d * TH + c * 512:d * TH + (c + 1) * 512]
                emit_ln_chunk(z_in, tc, xn_dst, lnB)
            xn_ready = S.mark(sem_v)
            S.op("vector", "tensor_copy", work[:, 15:23], sent_sb[:],
                 inc=[(sem_v, 1)])

            S.wait("tensor", sem_v, xn_ready)
            for tt in range(16):
                for d in range(8):
                    S.op("tensor", "matmul", aux[:, tt * 15:(tt + 1) * 15],
                         xn[:, d * TH + tt * 128:d * TH + (tt + 1) * 128],
                         gate_sb[:, d * 15:(d + 1) * 15],
                         start=(d == 0), stop=(d == 7),
                         inc=[(sem_t, 1)] if (tt == 15 and d == 7) else None)
            S.wait_cur("scalar", sem_t)
            S.op("scalar", "copy", out=lgt_all[:], in_=aux[:, 0:240],
                 inc=[(sem_s, 1)])
            comb_state["copy"] = S.mark(sem_s)
            S.wait_cur("vector", sem_s)
            for tt in range(16):
                lgt = lgt_all[:, tt * 15:(tt + 1) * 15]
                S.op("vector", "tensor_copy", work[:, 0:15], lgt,
                     inc=[(sem_v, 1)])
                S.wait_cur("vector", sem_v)
                S.op("vector", "max", out=m8[:], in_=work[:], inc=[(sem_v, 1)])
                S.wait_cur("vector", sem_v)
                S.op("vector", "tensor_copy", rep[:, 0:3], m8[:, 0:3])
                S.op("vector", "tensor_copy", rep[:, 3:8], sent_sb[:, 0:5],
                     inc=[(sem_v, 1)])
                S.wait_cur("vector", sem_v)
                S.op("vector", "match_replace", out=wout[:], in_to_replace=rep[:],
                     in_values=work[:], imm_value=NEG, inc=[(sem_v, 1)])
                S.wait_cur("vector", sem_v)
                S.op("vector", "tensor_single_scalar", sel[:], wout[:, 0:15],
                     -1e29, ALU.is_lt)
                S.op("vector", "tensor_scalar_mul", negm[:], m8[:, 0:1], -1.0,
                     inc=[(sem_v, 1)])
                S.wait_cur("scalar", sem_v)
                S.op("scalar", "activation", out=exm[:], in_=lgt,
                     func=AF.Exp, bias=negm[:], scale=1.0, inc=[(sem_s, 1)])
                S.wait_cur("vector", sem_s)
                S.op("vector", "tensor_tensor", exm[:], exm[:], sel[:], ALU.mult,
                     inc=[(sem_v, 1)])
                S.wait_cur("vector", sem_v)
                S.op("vector", "reduce_sum", gsum[:], exm[:], axis=AX.X,
                     inc=[(sem_v, 1)])
                S.wait_cur("vector", sem_v)
                S.op("vector", "reciprocal", ginv[:], gsum[:], inc=[(sem_v, 1)])
                S.wait_cur("vector", sem_v)
                S.op("vector", "tensor_single_scalar",
                     comb_tok[:, tt * 15:(tt + 1) * 15], exm[:], ginv[:],
                     ALU.mult, inc=[(sem_v, 1)])
            gating_done = S.mark(sem_v)

            w1_seq = [(u, jp, ab) for u in range(17) for jp in range(8)
                      for ab in range(2)]
            w2_seq = [(u, db) for u in range(17) for db in range(8)]
            w1_dma, w2_dma, w1_free_at, w2_free_at = {}, {}, {}, {}
            w1_next, w2_next = [0], [0]

            def pump_w1(n=1):
                for _ in range(n):
                    seq = w1_next[0]
                    if seq >= len(w1_seq):
                        return
                    u, jp, ab = w1_seq[seq]
                    kind, e = units[u]
                    if seq >= 6:
                        S.wait("gpsimd", sem_t, w1_free_at[seq - 6])
                    w1_dma[seq] = wstream.dma(
                        w1r[:, (seq % 6) * 1024:(seq % 6 + 1) * 1024],
                        w1_src(kind, e, jp, ab))
                    w1_next[0] += 1

            def pump_w2(n=1):
                for _ in range(n):
                    seq = w2_next[0]
                    if seq >= len(w2_seq):
                        return
                    u, db = w2_seq[seq]
                    kind, e = units[u]
                    if seq >= 4:
                        S.wait("gpsimd", sem_t, w2_free_at[seq - 4])
                    w2_dma[seq] = wstream.dma(
                        w2r[:, (seq % 4) * 1024:(seq % 4 + 1) * 1024],
                        w2_src(kind, e, db))
                    w2_next[0] += 1

            pump_w1(6)
            pump_w2(4)
            psY4 = [psY[0], psY[1], psA[0], psA[1]]
            psY4_free = {0: 0, 1: 0, 2: 0, 3: 0}
            crow_map = {}

            def emit_extract_all():
                S.wait("tensor", sem_v, gating_done)
                crow_dma = {}
                for e in range(NE):
                    for g in range(4):
                        S.wait_cur("tensor", sem_s)
                        for q in range(4):
                            tt = g * 4 + q
                            S.op("tensor", "matmul",
                                 aux[0:1, 512 + q * 128:512 + (q + 1) * 128],
                                 comb_tok[:, tt * 15 + e:tt * 15 + e + 1],
                                 ident_sb[:], start=True, stop=True,
                                 inc=[(sem_t, 1)] if q == 3 else None)
                        S.wait_cur("scalar", sem_t)
                        if e > 0:
                            zstream.wait_for("scalar", crow_dma[e - 1])
                        S.op("scalar", "copy",
                             out=combrow[:, g * 512:(g + 1) * 512],
                             in_=aux[0:1, 512:1024], inc=[(sem_s, 1)])
                    S.wait_cur("sync", sem_s)
                    crow_dma[e] = zstream.dma(comb_T[e:e + 1, :], combrow[:])
                crow_map.update(crow_dma)
                return crow_dma[NE - 1]

            def emit_yinit(last_crow):
                zstream.wait_for("tensor", last_crow)
                yfree = {0: psY4_free[0], 1: psY4_free[1]}
                for db in range(8):
                    for c in range(4):
                        par = c % 2
                        S.wait("tensor", sem_v, yfree[par])
                        S.op("tensor", "matmul", psY[par][:],
                             fc2bT_sb[:, db * 128:(db + 1) * 128],
                             comb_T[0:15, c * 512:(c + 1) * 512],
                             start=True, stop=True, inc=[(sem_t, 1)])
                        S.wait_cur("vector", sem_t)
                        S.op("vector", "tensor_single_scalar",
                             ysb[:, db * TH + c * 512:db * TH + (c + 1) * 512],
                             psY[par][:], sfc2b_sb[:, db:db + 1], ALU.add,
                             inc=[(sem_v, 1)])
                        yfree[par] = S.mark(sem_v)
                psY4_free[0] = yfree[0]
                psY4_free[1] = yfree[1]

            def emit_comb(u):
                kind, e = units[u]
                slot = comb_b[u % 2]
                S.wait("sync", sem_t, comb_state["mm"])
                zstream.wait_for("sync", crow_map[e])
                ci = zstream.dma(combrow[:], comb_T[e:e + 1, :])
                zstream.wait_for("tensor", ci)
                S.wait("scalar", sem_v,
                       comb_state["mult"][u % 2])
                for g in range(4):
                    S.wait("tensor", sem_s, comb_state["copy"])
                    S.op("tensor", "matmul", aux[:, 0:512], ones_rbf[:],
                         combrow[:, g * 512:(g + 1) * 512], start=True,
                         stop=True, inc=[(sem_t, 1)])
                    comb_state["mm"] = S.mark(sem_t)
                    S.wait("scalar", sem_t, comb_state["mm"])
                    S.op("scalar", "copy", out=slot[:, g * 512:(g + 1) * 512],
                         in_=aux[:, 0:512], inc=[(sem_s, 1)])
                    comb_state["copy"] = S.mark(sem_s)
                return S.mark(sem_s)

            comb_ready = {}
            psA3 = [psA[0], psA[1], psY[0]]
            psB3 = [psB[0], psB[1], psY[1]]
            psAB_free = {0: (0, 0), 1: (0, 0), 2: (0, 0)}
            gact_free = {0: 0, 1: 0, 2: 0}
            y_prog = {}
            deferred = [0]

            for u, (kind, e) in enumerate(units):
                npar = 2 if u == 0 else 3
                it = 0
                for jp in range(8):
                    seqA = u * 16 + 2 * jp
                    seqB = seqA + 1
                    bA, bB, gmc = bias_cols(kind, e, jp)
                    for c in range(4):
                        par = it % npar
                        if c == 0:
                            wstream.wait_for("tensor", w1_dma[seqA])
                            wstream.wait_for("tensor", w1_dma[seqB])
                        sB, vA = psAB_free[par]
                        S.wait("tensor", sem_s, sB)
                        S.wait("tensor", sem_v, vA)
                        if it < npar:
                            if par < 2:
                                S.wait("tensor", sem_v, psY4_free[2 + par])
                            else:
                                S.wait("tensor", sem_v, psY4_free[0])
                                S.wait("tensor", sem_v, psY4_free[1])
                        it += 1
                        wA = w1r[:, (seqA % 6) * 1024:(seqA % 6 + 1) * 1024]
                        wB = w1r[:, (seqB % 6) * 1024:(seqB % 6 + 1) * 1024]
                        for d in range(8):
                            S.op("tensor", "matmul", psA3[par][:],
                                 wA[:, d * 128:(d + 1) * 128],
                                 xn[:, d * TH + c * 512:d * TH + (c + 1) * 512],
                                 start=(d == 0), stop=(d == 7))
                        for d in range(8):
                            S.op("tensor", "matmul", psB3[par][:],
                                 wB[:, d * 128:(d + 1) * 128],
                                 xn[:, d * TH + c * 512:d * TH + (c + 1) * 512],
                                 start=(d == 0), stop=(d == 7),
                                 inc=[(sem_t, 1)] if d == 7 else None)
                        tmark = S.mark(sem_t)
                        S.wait("scalar", sem_t, tmark)
                        S.wait("scalar", sem_v, gact_free[par])
                        S.op("scalar", "activation", out=g_act[par][:],
                             in_=psB3[par][:], func=AF.Gelu, bias=bB,
                             scale=1.0, inc=[(sem_s, 1)])
                        smark = S.mark(sem_s)
                        S.wait("vector", sem_s, smark)
                        S.op("vector", "tensor_scalar", tmp_a[par][:],
                             psA3[par][:], bA, gmc, ALU.add, ALU.mult)
                        S.op("vector", "tensor_tensor",
                             a_sb[:, jp * TH + c * 512:jp * TH + (c + 1) * 512],
                             tmp_a[par][:], g_act[par][:], ALU.mult,
                             inc=[(sem_v, 1)])
                        psAB_free[par] = (smark, S.mark(sem_v))
                        gact_free[par] = S.mark(sem_v)
                    w1_free_at[seqA] = S.mark(sem_t)
                    w1_free_at[seqB] = S.mark(sem_t)
                    pump_w1(2)
                    if u == 0 and jp == 1 and deferred[0] == 0:
                        deferred[0] = 1
                        last_crow = emit_extract_all()
                    if u == 0 and jp == 4 and deferred[0] == 1:
                        deferred[0] = 2
                        emit_yinit(last_crow)
                        comb_ready[2] = emit_comb(2)
                a_done = S.mark(sem_v)

                if kind == 0:
                    S.wait("vector", sem_s, comb_ready[u])
                    for jp in range(8):
                        S.op("vector", "tensor_tensor",
                             a_sb[:, jp * TH:(jp + 1) * TH],
                             a_sb[:, jp * TH:(jp + 1) * TH],
                             comb_b[u % 2][:], ALU.mult,
                             inc=[(sem_v, 1)] if jp == 7 else None)
                    a_done = S.mark(sem_v)
                    comb_state["mult"][u % 2] = a_done
                if u + 1 < len(units) and units[u + 1][0] == 0:
                    comb_ready[u + 1] = emit_comb(u + 1)

                for db in range(8):
                    seq2 = u * 8 + db
                    wstream.wait_for("tensor", w2_dma[seq2])
                    if db == 0:
                        S.wait("tensor", sem_v, a_done)
                    w2a = w2r[:, (seq2 % 4) * 1024:(seq2 % 4 + 1) * 1024]
                    for c in range(4):
                        par = c
                        bank = psY4[par]
                        S.wait("tensor", sem_v, psY4_free[par])
                        for f in range(8):
                            S.op("tensor", "matmul", bank[:],
                                 w2a[:, f * 128:(f + 1) * 128],
                                 a_sb[:, f * TH + c * 512:f * TH + (c + 1) * 512],
                                 start=(f == 0), stop=(f == 7),
                                 inc=[(sem_t, 1)] if f == 7 else None)
                        S.wait_cur("vector", sem_t)
                        S.op("vector", "tensor_tensor",
                             ysb[:, db * TH + c * 512:db * TH + (c + 1) * 512],
                             ysb[:, db * TH + c * 512:db * TH + (c + 1) * 512],
                             bank[:], ALU.add, inc=[(sem_v, 1)])
                        psY4_free[par] = S.mark(sem_v)
                    w2_free_at[seq2] = S.mark(sem_t)
                    pump_w2(1)
                    y_prog[db] = S.mark(sem_v)

            st_base = n_stores[0]
            res_v = {}
            for d8 in range(8):
                par = d8 % 2
                rio = stg[:, par * 2048:(par + 1) * 2048]
                if d8 >= 2:
                    S.wait("sync", sem_v, res_v[d8 - 2])
                    S.wait("sync", sem_st, 16 * (st_base + d8 - 1))
                else:
                    S.wait("sync", sem_v, xn_ready)
                idx = zstream.dma(rio, z_in[d8 * 128:(d8 + 1) * 128,
                                            half * TH:(half + 1) * TH])
                zstream.wait_for("vector", idx)
                S.wait("vector", sem_v, y_prog[d8])
                S.op("vector", "scalar_tensor_tensor", rio,
                     ysb[:, d8 * TH:(d8 + 1) * TH], 0.5, rio,
                     ALU.mult, ALU.add, inc=[(sem_v, 1)])
                res_v[d8] = S.mark(sem_v)
                S.wait("sync", sem_v, res_v[d8])
                store(z_out[d8 * 128:(d8 + 1) * 128, half * TH:(half + 1) * TH],
                      rio)
        light_barrier()
        ff.close()

    def emit_attn(z_in, z_out, tag="at"):
        slopes = _slopes16()
        ff = contextlib.ExitStack()

        def fsb(name, shape, dtype):
            return ff.enter_context(nc.sbuf_tensor(f"{tag}{name}", list(shape), dtype))

        def fps(name, shape):
            return ff.enter_context(nc.psum_tensor(f"{tag}{name}", list(shape), F32))

        xn = fsb("xn", (128, 8 * T), BF16)
        osb = fsb("o", (128, 8 * T), BF16)
        vbuf = fsb("vb", (128, 32 * 260), BF16)
        qb = fsb("qb", (128, T), BF16)
        kb = fsb("kb", (128, T), BF16)
        pt = [fsb(f"pt{i}", (128, 512), BF16) for i in range(4)]
        invhl = fsb("invhl", (64, T), BF16)
        wqkv = fsb("wqkv", (128, 1024), BF16)
        wo_r = fsb("wo_r", (128, 2 * 1024), BF16)
        wvg = wo_r[:]
        stg = fsb("stg", (128, 4096), F32)
        bvb = fsb("bvb", (128, E), BF16)
        masks_sb = fsb("masks", (128, 4 * 512), BF16)
        mb = fsb("mb", (128, 512), F32)
        rstdb = fsb("rstdb", (128, 512), F32)
        tmpf = fsb("tmpf", (128, 512), F32)
        mrow = rstdb[0:1, 0:512]
        vrow = mb[0:1, 0:512]
        bvrow_l = stg[0:1, 0:E]

        psS = [fps(f"psS{i}", (128, 512)) for i in range(4)]
        psO = fps("psO", (128, 512))
        psQ = fps("psQ", (128, 512))
        psP = [fps(f"psP{i}", (128, 512)) for i in range(2)]

        class _Aux:
            def __getitem__(self, key):
                p, c = key
                if isinstance(c, slice) and c.start == 512:
                    return psS[1][p, c.start - 512:c.stop - 512]
                return psS[0][p, c]
        _aux = _Aux()

        lnB = dict(
            zbf=[osb[:, 0:4096], osb[:, 8192:12288]],
            zsq=[osb[:, 4096:8192], osb[:, 12288:16384]],
            psBc=[(psS[0][:], psS[1][:]), (psS[2][:], psS[3][:])],
            rows=[(rstdb[0:1, 0:512], mb[0:1, 0:512], tmpf[0:1, 0:512]),
                  (stg[0:1, 0:512], stg[0:1, 512:1024],
                   stg[0:1, 1024:1536])],
            lhs=[(ones_r32[0:1, :], ones_r32[0:1, :]),
                 (ones_r32[0:1, :], ones_r32[0:1, :])],
            tmp=[mb[:], rstdb[:]], mv={}, mt={})

        light_barrier()
        wstream.dma(masks_sb[:].rearrange("p (r q) -> p r q", r=4),
                    dt["masks"].rearrange("r p q -> p r q"))
        wstream.dma(bvrow_l[:], dt["bvrow"])
        wstream.wait_all("vector")
        wstream.wait_all("tensor")
        S.wait_cur("tensor", sem_v)
        S.op("tensor", "matmul", psS[0][:], ones_r32[0:1, :],
             bvrow_l[:, 0:512], start=True, stop=True)
        S.op("tensor", "matmul", psS[1][:], ones_r32[0:1, :],
             bvrow_l[:, 512:1024], start=True, stop=True,
             inc=[(sem_t, 1)])
        S.wait_cur("scalar", sem_t)
        S.op("scalar", "copy", out=bvb[:, 0:512], in_=psS[0][:])
        S.op("scalar", "copy", out=bvb[:, 512:1024], in_=psS[1][:],
             inc=[(sem_s, 1)])
        S.wait_cur("tensor", sem_s)

        psVp = [psP[1], psQ]
        psVp_hold = {0: {"v": 0, "s": 0}, 1: {"v": 0, "s": 0}}
        psO_free = {0: 0, 1: 0}
        pt_free = {0: 0, 1: 0, 2: 0, 3: 0}
        psS_war = {0: 0, 1: 0}
        pend_norm = [None]
        pend_epi = [None]
        last_S_mark = 0

        def flush_epi():
            if pend_epi[0] is None:
                return
            e_ps, e_ob, e_dt, e_pb, e_qc, e_pv = pend_epi[0]
            pend_epi[0] = None
            S.wait("vector", sem_t, e_pv)
            S.op("vector", "tensor_copy",
                 invhl[0:1, e_qc * 512:(e_qc + 1) * 512], e_ps[64:65, :])
            S.op("vector", "tensor_tensor",
                 invhl[32:33, e_qc * 512:(e_qc + 1) * 512],
                 e_ps[64:65, :],
                 invhl[0:1, e_qc * 512:(e_qc + 1) * 512],
                 ALU.subtract)
            S.op("vector", "tensor_copy",
                 osb[e_pb:e_pb + 64,
                     e_dt * T + e_qc * 512:e_dt * T + (e_qc + 1) * 512],
                 e_ps[0:64, :], inc=[(sem_v, 1)])
            psO_free[e_ob] = S.mark(sem_v)

        def emit_norm():
            if pend_norm[0] is None:
                return
            ndt, npb = pend_norm[0]
            pend_norm[0] = None
            S.wait_cur("tensor", sem_v)
            for nqc in range(8):
                par = nqc % 2
                bank = psS[2 + par]
                S.wait("tensor", sem_s, pt_free[2 + par])
                S.wait("tensor", sem_v, psS_war[par])
                S.op("tensor", "matmul", bank[0:64, :],
                     selq_sb[0:1, 0:64],
                     invhl[0:1, nqc * 512:(nqc + 1) * 512],
                     start=True, stop=True)
                S.op("tensor", "matmul", bank[64:128, :],
                     selq_sb[32:33, 0:64],
                     invhl[32:33, nqc * 512:(nqc + 1) * 512],
                     start=True, stop=True, inc=[(sem_t, 1)])
                nm = S.mark(sem_t)
                S.wait("vector", sem_t, nm)
                S.op("vector", "tensor_copy", tmpf[npb:npb + 64, :],
                     bank[64:128, :])
                S.op("vector", "tensor_tensor", tmpf[npb:npb + 64, :],
                     tmpf[npb:npb + 64, :], bank[0:64, :], ALU.add)
                S.op("vector", "reciprocal", tmpf[npb:npb + 64, :],
                     tmpf[npb:npb + 64, :])
                S.op("vector", "tensor_tensor",
                     osb[npb:npb + 64,
                         ndt * T + nqc * 512:ndt * T + (nqc + 1) * 512],
                     osb[npb:npb + 64,
                         ndt * T + nqc * 512:ndt * T + (nqc + 1) * 512],
                     tmpf[npb:npb + 64, :], ALU.mult,
                     inc=[(sem_v, 1)])
                psS_war[par] = S.mark(sem_v)
        S.op("vector", "memset", qb[64:70, :], 1.0)
        S.op("vector", "memset", kb[64:70, :], 1.0)
        S.op("vector", "memset", vbuf[:], 1.0, inc=[(sem_v, 1)])
        init_done = S.mark(sem_v)
        S.wait("gpsimd", sem_v, init_done)
        vw0_idx = wstream.dma(wvg[:], dt["wvs"][0].rearrange("p d j -> p (d j)"))
        wstream.wait_for("tensor", vw0_idx)
        for tc in range(8):
            def xn_dst(d, tc=tc):
                return xn[:, d * T + tc * 512:d * T + (tc + 1) * 512]
            emit_ln_chunk(z_in, tc, xn_dst, lnB)
            for tt in range(4 * tc, 4 * tc + 4):
                par = tt % 2
                pb = psVp[par]
                S.wait("tensor", sem_v, lnB["mv"][tc])
                S.wait("tensor", sem_v, psVp_hold[par]["v"])
                S.wait("tensor", sem_s, psVp_hold[par]["s"])
                for d in range(8):
                    S.op("tensor", "matmul", pb[:, 0:256],
                         xn[:, d * T + tt * 128:d * T + (tt + 1) * 128],
                         wvg[:, d * 256:(d + 1) * 256],
                         start=(d == 0), stop=(d == 7),
                         inc=[(sem_t, 1)] if d == 7 else None)
                S.wait_cur("vector", sem_t)
                dst = vbuf[:, tt * 260:(tt + 1) * 260].rearrange(
                    "p (hh c) -> p hh c", hh=4)[:, :, 0:64]
                S.op("vector", "tensor_tensor", dst,
                     pb[:, 0:256].rearrange("p (hh c) -> p hh c", hh=4),
                     bvb[:, 0:256].rearrange("p (hh c) -> p hh c", hh=4),
                     ALU.add, inc=[(sem_v, 1)])
                psVp_hold[par]["v"] = S.mark(sem_v)
        xn_ready = S.mark(sem_v)

        def kt_min(h, qc):
            d = int(WIN_C / slopes[h])
            return max(0, (qc * 512 - d - 127) // 128)

        for g in range(4):
            if g > 0:
                vw_idx = wstream.dma(wvg[:],
                                     dt["wvs"][g].rearrange("p d j -> p (d j)"))
                wstream.wait_for("tensor", vw_idx)
                for tt in range(32):
                    par = tt % 2
                    pb = psVp[par]
                    S.wait("tensor", sem_v, psVp_hold[par]["v"])
                    S.wait("tensor", sem_s, psVp_hold[par]["s"])
                    for d in range(8):
                        S.op("tensor", "matmul", pb[:, 0:256],
                             xn[:, d * T + tt * 128:d * T + (tt + 1) * 128],
                             wvg[:, d * 256:(d + 1) * 256],
                             start=(d == 0), stop=(d == 7),
                             inc=[(sem_t, 1)] if d == 7 else None)
                    S.wait_cur("vector", sem_t)
                    dst = vbuf[:, tt * 260:(tt + 1) * 260].rearrange(
                        "p (hh c) -> p hh c", hh=4)[:, :, 0:64]
                    S.op("vector", "tensor_tensor", dst,
                         pb[:, 0:256].rearrange("p (hh c) -> p hh c", hh=4),
                         bvb[:, g * 256:(g + 1) * 256].rearrange(
                             "p (hh c) -> p hh c", hh=4),
                         ALU.add, inc=[(sem_v, 1)])
                    psVp_hold[par]["v"] = S.mark(sem_v)

            for h in range(4 * g, 4 * g + 4):
                dt_h = h // 2
                pbase = (h % 2) * 64
                S.wait("gpsimd", sem_t, last_S_mark)
                qk_idx = wstream.dma(wqkv[:],
                                     dt["wqks"][h].rearrange("p d j -> p (d j)"))
                aq_idx = wstream.dma(qb[64:67, :], dt["alq"][h])
                ak_idx = wstream.dma(kb[67:70, :], dt["alk"][h])
                wstream.wait_for("tensor", qk_idx)
                for cc in range(8):
                    par = cc % 2
                    pb = psVp[par]
                    S.wait("tensor", sem_s, psVp_hold[par]["s"])
                    S.wait("tensor", sem_v, psVp_hold[par]["v"])
                    for d in range(8):
                        S.op("tensor", "matmul", pb[:],
                             wqkv[:, d * 128:(d + 1) * 128],
                             xn[:, d * T + cc * 512:d * T + (cc + 1) * 512],
                             start=(d == 0), stop=(d == 7),
                             inc=[(sem_t, 1)] if d == 7 else None)
                    pj_mm = S.mark(sem_t)
                    S.wait("scalar", sem_t, pj_mm)
                    S.op("scalar", "activation",
                         out=qb[0:64, cc * 512:(cc + 1) * 512],
                         in_=pb[0:64, :], func=AF.Identity,
                         bias=bqk_sb[0:64, h:h + 1], scale=1.0)
                    S.op("scalar", "activation",
                         out=kb[0:64, cc * 512:(cc + 1) * 512],
                         in_=pb[64:128, :], func=AF.Identity,
                         bias=bqk_sb[64:128, h:h + 1],
                         scale=1.0, inc=[(sem_s, 1)])
                    psVp_hold[par]["s"] = S.mark(sem_s)
                proj_done = S.mark(sem_s)
                wstream.wait_for("tensor", aq_idx)
                wstream.wait_for("tensor", ak_idx)
                emit_norm()
                S.wait("tensor", sem_s, proj_done)

                for qc in range(8):
                    kths = list(range(kt_min(h, qc), 4 * qc + 4))
                    nk = len(kths)
                    ob = qc % 2
                    psOb = psO if ob == 0 else psP[0]
                    emarks = {}
                    LA = 2
                    for idx in range(nk + LA):
                        if idx == 2:
                            flush_epi()
                        if idx < nk:
                            kt = kths[idx]
                            par = idx % 4
                            S.wait("tensor", sem_s, pt_free[par])
                            if par >= 2:
                                S.wait("tensor", sem_v, psS_war[par - 2])
                            S.op("tensor", "matmul", psS[par][:],
                                 kb[0:70, kt * 128:(kt + 1) * 128],
                                 qb[0:70, qc * 512:(qc + 1) * 512],
                                 start=True, stop=True, inc=[(sem_t, 1)])
                            s_mm = S.mark(sem_t)
                            last_S_mark = s_mm
                            r = kt * 128 - qc * 512
                            if r >= 0:
                                ri = r // 128
                                S.wait("vector", sem_t, s_mm)
                                S.op("vector", "tensor_tensor", psS[par][:],
                                     psS[par][:],
                                     masks_sb[:, ri * 512:(ri + 1) * 512],
                                     ALU.add, inc=[(sem_v, 1)])
                                S.wait_cur("scalar", sem_v)
                            else:
                                S.wait("scalar", sem_t, s_mm)
                            S.op("scalar", "activation", out=pt[par][:],
                                 in_=psS[par][:], func=AF.Exp,
                                 inc=[(sem_s, 1)])
                            emarks[idx] = S.mark(sem_s)
                        j = idx - LA
                        if j >= 0:
                            kt_j = kths[j]
                            par_j = j % 4
                            S.wait("tensor", sem_s, emarks[j])
                            if j == 0:
                                S.wait("tensor", sem_v, psO_free[ob])
                            S.op("tensor", "matmul", psOb[0:65, :],
                                 vbuf[:, kt_j * 260 + (h % 4) * 65:
                                      kt_j * 260 + (h % 4) * 65 + 65],
                                 pt[par_j][:], start=(j == 0),
                                 stop=(j == nk - 1),
                                 inc=[(sem_t, 1)] if j == nk - 1 else None)
                            pt_free[par_j] = emarks[j]
                    pend_epi[0] = (psOb, ob, dt_h, pbase, qc,
                                   S.mark(sem_t))
                flush_epi()
                pend_norm[0] = (dt_h, pbase)
        emit_norm()
        o_done = S.mark(sem_v)

        S.wait("tensor", sem_v, o_done)
        S.wait_cur("gpsimd", sem_t)
        wo_dma = {}
        wo_seq = 0
        wo_free = {}

        def pump_wo(jt_seq):
            nonlocal wo_seq
            if jt_seq >= 2:
                S.wait("gpsimd", sem_t, wo_free[jt_seq - 2])
            wo_dma[jt_seq] = wstream.dma(
                wo_r[:, (jt_seq % 2) * 1024:(jt_seq % 2 + 1) * 1024],
                dt["wos"][jt_seq % 8].rearrange("p d j -> p (d j)"))

        pump_wo(0)
        pump_wo(1)
        psW_free = {0: 0, 1: 0}
        ring = [mb[:], rstdb[:], tmpf[:]]
        ring_v = {}
        ld_idx = {}
        st_of = {}

        def emit_rload(k):
            tcx, jtx = divmod(k, 8)
            if k >= 3:
                S.wait("sync", sem_v, ring_v[k - 3])
            else:
                S.wait("sync", sem_v, o_done)
            ld_idx[k] = zstream.dma(
                ring[k % 3], z_in[jtx * 128:(jtx + 1) * 128,
                                  tcx * 512:(tcx + 1) * 512])

        for k in range(3):
            emit_rload(k)
        for tc in range(8):
            for jt in range(8):
                jt_seq = tc * 8 + jt
                wstream.wait_for("tensor", wo_dma[jt_seq])
                par = jt % 2
                woa = wo_r[:, (jt_seq % 2) * 1024:(jt_seq % 2 + 1) * 1024]
                S.wait("tensor", sem_v, psW_free[par])
                for d in range(8):
                    S.op("tensor", "matmul", psS[par][:],
                         woa[:, d * 128:(d + 1) * 128],
                         osb[:, d * T + tc * 512:d * T + (tc + 1) * 512],
                         start=(d == 0), stop=(d == 7),
                         inc=[(sem_t, 1)] if d == 7 else None)
                wo_free[jt_seq] = S.mark(sem_t)
                if jt_seq + 2 < 64:
                    pump_wo(jt_seq + 2)
                zstream.wait_for("vector", ld_idx[jt_seq])
                if tc > 0:
                    S.wait("vector", sem_st,
                           16 * (st_of[jt_seq - 8] + 1))
                S.wait("vector", sem_t, wo_free[jt_seq])
                S.op("vector", "scalar_tensor_tensor",
                     stg[:, jt * 512:(jt + 1) * 512], psS[par][:],
                     bo_sb[:, jt:jt + 1], ring[jt_seq % 3],
                     ALU.add, ALU.add, inc=[(sem_v, 1)])
                ring_v[jt_seq] = S.mark(sem_v)
                psW_free[par] = ring_v[jt_seq]
                S.wait("sync", sem_v, ring_v[jt_seq])
                st_of[jt_seq] = n_stores[0]
                store(z_out[jt * 128:(jt + 1) * 128,
                            tc * 512:(tc + 1) * 512],
                      stg[:, jt * 512:(jt + 1) * 512])
                if jt_seq + 3 < 64:
                    emit_rload(jt_seq + 3)
        light_barrier()
        ff.close()

    block = es.enter_context(nc.Block())
    for p in parts:
        z_in, z_out = io[p]
        if p in ("ffn1", "ffn2"):
            emit_ffn(z_in, z_out, "f1" if p == "ffn1" else "f2")
        else:
            emit_attn(z_in, z_out)
    S.wait("sync", sem_st, 16 * n_stores[0])
    S.replay(block)
    es.close()
    return nc


def _host_shared(i):
    f32 = np.float32
    out = {}
    fc1_w = np.asarray(i["L0_ffn_fc1_w"], f32)
    out["w1s"] = np.ascontiguousarray(
        fc1_w.reshape(NE, 16, 128, 8, 128).transpose(0, 1, 4, 3, 2))
    fc2_w = np.asarray(i["L0_ffn_fc2_w"], f32)
    out["w2s"] = np.ascontiguousarray(
        fc2_w.reshape(NE, 8, 128, 8, 128).transpose(0, 1, 4, 3, 2))
    s1 = np.asarray(i["L0_ffn_s_fc1_w"], f32)
    out["ws1s"] = np.ascontiguousarray(
        s1.reshape(32, 128, 8, 128).transpose(0, 3, 2, 1))
    s2 = np.asarray(i["L0_ffn_s_fc2_w"], f32)
    out["ws2s"] = np.ascontiguousarray(
        s2.reshape(8, 128, 16, 128).transpose(0, 3, 2, 1))
    gw = np.asarray(i["L0_ffn_gate_w"], f32)
    out["gws"] = np.ascontiguousarray(gw.reshape(15, 8, 128).transpose(2, 1, 0))
    wq8 = np.asarray(i["L0_attn_Wq"], f32) * 0.125
    wk = np.asarray(i["L0_attn_Wk"], f32)
    qpart = wq8.reshape(16, 64, 8, 128).transpose(0, 3, 2, 1)
    kpart = wk.reshape(16, 64, 8, 128).transpose(0, 3, 2, 1)
    out["wqks"] = np.ascontiguousarray(
        np.concatenate([qpart, kpart], axis=3))
    wv = np.asarray(i["L0_attn_Wv"], f32)
    out["wvs"] = np.ascontiguousarray(
        wv.reshape(4, 256, 8, 128).transpose(0, 3, 2, 1))
    wo = np.asarray(i["L0_attn_Wo"], f32)
    out["wos"] = np.ascontiguousarray(
        wo.reshape(8, 128, 8, 128).transpose(0, 3, 2, 1))
    out["fc1b"] = np.ascontiguousarray(
        np.asarray(i["L0_ffn_fc1_b"], f32).reshape(NE, 16, 128)
        .transpose(2, 0, 1).reshape(128, NE * 16))
    out["fc2bT"] = np.ascontiguousarray(np.asarray(i["L0_ffn_fc2_b"], f32))
    out["gmult"] = np.ascontiguousarray(
        np.asarray(i["L0_ffn_geglu_mult"], f32).reshape(NE, 8, 128)
        .transpose(2, 0, 1).reshape(128, NE * 8))
    out["sfc1b"] = np.ascontiguousarray(
        np.asarray(i["L0_ffn_s_fc1_b"], f32).reshape(32, 128).T)
    out["sgm"] = np.ascontiguousarray(
        np.asarray(i["L0_ffn_s_geglu_mult"], f32).reshape(16, 128).T)
    out["sfc2b"] = np.ascontiguousarray(
        np.asarray(i["L0_ffn_s_fc2_b"], f32).reshape(8, 128).T)
    bq8 = np.asarray(i["L0_attn_bq"], f32).reshape(16, 64) * 0.125
    bk_ = np.asarray(i["L0_attn_bk"], f32).reshape(16, 64)
    out["bqkv"] = np.ascontiguousarray(
        np.concatenate([bq8, bk_], axis=1).T)
    selq = np.zeros((128, 128), f32)
    selq[0, 0:64] = 1.0
    selq[32, 0:64] = 1.0
    out["selqv"] = selq
    out["bvrow"] = np.asarray(i["L0_attn_bv"], f32).reshape(1, E)
    out["bov"] = np.ascontiguousarray(np.asarray(i["L0_attn_bo"], f32).reshape(8, 128).T)

    slopes = _slopes16().astype(np.float64)
    pos = np.arange(T, dtype=np.float64)
    alq = np.zeros((H, 3, T), np.float32)
    alk = np.zeros((H, 3, T), np.float32)
    for h in range(H):
        alq[h] = _tri_split(-slopes[h] * pos)
        alk[h] = _tri_split(slopes[h] * pos)
    out["alq"] = alq
    out["alk"] = alk
    masks = np.zeros((4, 128, 512), np.float32)
    kk = np.arange(128)[:, None]
    qq = np.arange(512)[None, :]
    for r in range(4):
        masks[r] = np.where(kk + r * 128 <= qq, 0.0, NEG)
    out["masks"] = masks
    out["identb"] = np.eye(128, dtype=np.float32)
    sent = np.array([-(1e30 * (1.0 + 0.01 * k)) for k in range(1, 9)], np.float32)
    out["sentv"] = np.tile(sent, (128, 1))
    return out


_CACHED = {}


def kernel(**inputs):
    parts = _CACHED.get("parts", ("ffn1", "attn", "ffn2"))
    key = ("nc", parts)
    if key not in _CACHED:
        _CACHED[key] = build_program(parts)
    nc = _CACHED[key]
    shared = _host_shared(inputs)
    x = np.asarray(inputs["L0_attn_x"], np.float32)
    in_maps = []
    for b in range(8):
        m = dict(shared)
        m["xT"] = np.ascontiguousarray(x[b].T)
        in_maps.append(m)
    res = run_bass_kernel_spmd(nc, in_maps, core_ids=list(range(8)))
    out = np.stack([np.asarray(r["outT"]).T for r in res.results])
    return np.ascontiguousarray(out.astype(np.float32))
